# version 10
# speedup vs baseline: 1.4024x; 1.0005x over previous
import os
import numpy as np
import ml_dtypes
from contextlib import ExitStack

import concourse.bass as bass
import concourse.mybir as mybir
from concourse.bass_utils import run_bass_kernel_spmd
from concourse.library_overlay import lower_extended_insts
from concourse.library_config import sparse_gather as SG_LIB, mlp as MLP_LIB
from concourse.aot_env import allow_ici_all_to_all

F32 = mybir.dt.float32
BF16 = mybir.dt.bfloat16
F16 = mybir.dt.float16
I16 = mybir.dt.int16
U32 = mybir.dt.uint32
AF = mybir.ActivationFunctionType
ALU = mybir.AluOpType
AX = mybir.AxisListType

T = 8192
D = 1024
CAP = 2048
FF = 4096
Q = 2048
NSTEP = 9
KC = 64
NSB = 32
NT = 16
RG = [[0, 1, 2, 3], [4, 5, 6, 7]]

os.environ.setdefault("TRN_ALLOW_ICI_ALL_TO_ALL", "1")


class Sem:
    def __init__(self, h):
        self.h = h
        self.n = 0

    def inc(self, inst, k=1):
        inst.then_inc(self.h, k)
        self.n += k
        return self.n


def build_nc():
    nc = bass.Bass("TRN2", target_bir_lowering=False, debug=False,
                   detect_race_conditions=True)

    x_my = nc.dram_tensor("x_my", [Q, D], F32, kind="ExternalInput").ap()
    wqT_d = nc.dram_tensor("wqT", [D, 128], F32, kind="ExternalInput").ap()
    wkT_d = nc.dram_tensor("wkT", [D, 128], F32, kind="ExternalInput").ap()
    bqp_d = nc.dram_tensor("bqp", [128, 1], F32, kind="ExternalInput").ap()
    bkp_d = nc.dram_tensor("bkp", [128, 1], F32, kind="ExternalInput").ap()
    wvT_d = nc.dram_tensor("wvT", [D, 256], F32, kind="ExternalInput").ap()
    bv_d = nc.dram_tensor("bv_my", [1, 256], F32, kind="ExternalInput").ap()
    gamma_d = nc.dram_tensor("gamma_in", [1, 1], F32, kind="ExternalInput").ap()
    rolev_d = nc.dram_tensor("rolev", [1, 1], F32, kind="ExternalInput").ap()
    rolei_d = nc.dram_tensor("rolei", [1, 1], mybir.dt.int32, kind="ExternalInput").ap()
    gwT_d = nc.dram_tensor("gwT", [D, 8], F32, kind="ExternalInput").ap()
    w1_d = [nc.dram_tensor(f"w1_{i}", [D, FF], F32, kind="ExternalInput").ap() for i in range(2)]
    w2_d = [nc.dram_tensor(f"w2_{i}", [FF, D], F32, kind="ExternalInput").ap() for i in range(2)]
    out_my = nc.dram_tensor("out_my", [Q, D], F32, kind="ExternalOutput").ap()

    xn_stash = nc.dram_tensor("xn_stash", [Q, D], F32, kind="Internal")
    pr_stash = nc.dram_tensor("pr_stash", [128, 8, 16], F32, kind="Internal")
    ag_xn_i = nc.dram_tensor("ag_xn_i", [Q, D], BF16, kind="Internal")
    ag_xn = nc.dram_tensor("ag_xn", [T, D], BF16, kind="Internal")
    ag_xnt_i4 = [nc.dram_tensor(f"ag_xnt_i{j}", [8, 128, 512], BF16, kind="Internal")
                 for j in range(4)]
    ag_xnt4 = [nc.dram_tensor(f"ag_xnt{j}", [32, 128, 512], BF16, kind="Internal")
               for j in range(4)]
    ag_pr_i = nc.dram_tensor("ag_pr_i", [1024, 16], F32, kind="Internal")
    ag_pr = nc.dram_tensor("ag_pr", [4096, 16], F32, kind="Internal")
    ag_att_i = nc.dram_tensor("ag_att_i", [256, T], BF16, kind="Internal")
    ag_att = nc.dram_tensor("ag_att", [1024, T], BF16, kind="Internal")
    ag_zi_i = nc.dram_tensor("ag_zi_i", [256, 64], F32, kind="Internal")
    ag_zi = nc.dram_tensor("ag_zi", [1024, 64], F32, kind="Internal")
    ag_ffn_i = nc.dram_tensor("ag_ffn_i", [2 * (CAP + 1), D], BF16, kind="Internal")
    ag_ffn = nc.dram_tensor("ag_ffn", [8 * (CAP + 1), D], BF16, kind="Internal")
    ag_inv_i = nc.dram_tensor("ag_inv_i", [32, 512], I16, kind="Internal")
    ag_inv = nc.dram_tensor("ag_inv", [128, 512], I16, kind="Internal")

    ident_f32 = nc.inline_tensor(np.eye(128, dtype=np.float32), "k_ident_f32")
    ident_bf = nc.inline_tensor(np.eye(128, dtype=np.float32).astype(ml_dtypes.bfloat16), "k_ident_bf")
    iota15_c = nc.inline_tensor(np.arange(1, 16, dtype=np.float32).reshape(1, 15), "k_iota15")
    o_1x128 = nc.inline_tensor(np.ones((1, 128), np.float32), "k_o1x128")
    o_1x128b = nc.inline_tensor(np.ones((1, 128), np.float32).astype(ml_dtypes.bfloat16), "k_o1x128b")
    o_128x1 = nc.inline_tensor(np.ones((128, 1), np.float32), "k_o128x1")
    o128_bf = nc.inline_tensor(np.ones((128, 1), np.float32).astype(ml_dtypes.bfloat16), "k_o128bf")
    o_16x1 = nc.inline_tensor(np.ones((16, 1), np.float32), "k_o16x1")
    o_1x16 = nc.inline_tensor(np.ones((1, 16), np.float32), "k_o1x16")
    u16_cn = nc.inline_tensor(np.triu(np.ones((16, 16), np.float32), 1), "k_u16")
    iota_sp1 = nc.inline_tensor(
        (np.arange(T, dtype=np.float32) + 1.0).reshape(512, 16).T.copy(), "k_iotasp")
    scv = np.ones((128, 1), np.float32)
    scv[0:16] = 0.25
    scv[64:80] = 0.25
    qk_scale = nc.inline_tensor(scv, "k_qksc")
    onesrow = nc.inline_tensor(
        np.ones((1, T), np.float32).astype(ml_dtypes.bfloat16), "k_onesrow")

    ctx = ExitStack()
    with ctx:
        en = ctx.enter_context
        c_id32 = en(nc.sbuf_tensor("c_id32", [128, 128], F32))
        c_idbf = en(nc.sbuf_tensor("c_idbf", [128, 128], BF16))
        c_i15 = en(nc.sbuf_tensor("c_i15", [1, 15], F32))
        c_o1x128 = en(nc.sbuf_tensor("c_o1x128", [1, 128], F32))
        c_o1x128b = en(nc.sbuf_tensor("c_o1x128b", [1, 128], BF16))
        c_o128x1 = en(nc.sbuf_tensor("c_o128x1", [128, 1], F32))
        c_o128bf = en(nc.sbuf_tensor("c_o128bf", [128, 1], BF16))
        c_o16x1 = en(nc.sbuf_tensor("c_o16x1", [16, 1], F32))
        c_o1x16 = en(nc.sbuf_tensor("c_o1x16", [1, 16], F32))
        c_u16 = en(nc.sbuf_tensor("c_u16", [16, 16], F32))
        c_iotasp = en(nc.sbuf_tensor("c_iotasp", [16, 512], F32))
        c_qksc = en(nc.sbuf_tensor("c_qksc", [128, 1], F32))
        gammav = en(nc.sbuf_tensor("gammav", [128, 1], F32))
        bqp_s = en(nc.sbuf_tensor("bqp_s", [128, 1], F32))
        bqp_sc = en(nc.sbuf_tensor("bqp_sc", [128, 1], F32))
        bkp_s = en(nc.sbuf_tensor("bkp_s", [128, 1], F32))
        bv_s = en(nc.sbuf_tensor("bv_s", [1, 256], F32))
        bv_b = en(nc.sbuf_tensor("bv_b", [1, 256], BF16))
        gam1 = en(nc.sbuf_tensor("gam1", [1, 1], F32))
        role1 = en(nc.sbuf_tensor("role1", [1, 1], F32))
        rolei_s = en(nc.sbuf_tensor("rolei_s", [1, 1], mybir.dt.int32))
        eoff = en(nc.sbuf_tensor("eoff", [1, 2], F32))
        eo16 = en(nc.sbuf_tensor("eo16", [16, 2], F32))
        zrow = en(nc.sbuf_tensor("zrow", [1, D], BF16))

        qv = Sem(en(nc.semaphore("qv")))
        qs = Sem(en(nc.semaphore("qs")))
        qt = Sem(en(nc.semaphore("qt")))
        dS = Sem(en(nc.semaphore("dS")))
        dG = Sem(en(nc.semaphore("dG")))
        ccS = Sem(en(nc.semaphore("ccS")))
        dO = Sem(en(nc.semaphore("dO")))
        sSQ = Sem(en(nc.semaphore("sSQ")))
        sLN = Sem(en(nc.semaphore("sLN")))
        sEV = Sem(en(nc.semaphore("sEV")))
        sGM = Sem(en(nc.semaphore("sGM")))
        sGX = Sem(en(nc.semaphore("sGX")))
        sPR = Sem(en(nc.semaphore("sPR")))
        sQKm = Sem(en(nc.semaphore("sQKm")))
        sQKe = Sem(en(nc.semaphore("sQKe")))
        sVm = Sem(en(nc.semaphore("sVm")))
        sVe = Sem(en(nc.semaphore("sVe")))
        sXU = Sem(en(nc.semaphore("sXU")))
        pSC = Sem(en(nc.semaphore("pSC")))
        pLK = Sem(en(nc.semaphore("pLK")))
        pEX = Sem(en(nc.semaphore("pEX")))
        pBV = Sem(en(nc.semaphore("pBV")))
        aEV = Sem(en(nc.semaphore("aEV")))
        dAT = Sem(en(nc.semaphore("dAT")))
        rA = Sem(en(nc.semaphore("rA")))
        rB = Sem(en(nc.semaphore("rB")))
        rC = Sem(en(nc.semaphore("rC")))
        rD = Sem(en(nc.semaphore("rD")))
        rM = Sem(en(nc.semaphore("rM")))
        gSG = Sem(en(nc.semaphore("gSG")))
        dRI = Sem(en(nc.semaphore("dRI")))
        fH = Sem(en(nc.semaphore("fH")))
        fHm = Sem(en(nc.semaphore("fHm")))
        fW2 = Sem(en(nc.semaphore("fW2")))
        fY = Sem(en(nc.semaphore("fY")))
        dFF = Sem(en(nc.semaphore("dFF")))
        fDone = Sem(en(nc.semaphore("fDone")))
        zTK = Sem(en(nc.semaphore("zTK")))
        cpS = Sem(en(nc.semaphore("cpS")))
        dW = Sem(en(nc.semaphore("dW")))
        dG2 = Sem(en(nc.semaphore("dG2")))
        sE8 = Sem(en(nc.semaphore("sE8")))
        pBC = Sem(en(nc.semaphore("pBC")))
        dAS = Sem(en(nc.semaphore("dAS")))
        sVAR = Sem(en(nc.semaphore("sVAR")))
        aTR = Sem(en(nc.semaphore("aTR")))
        aAC = Sem(en(nc.semaphore("aAC")))
        aXV = Sem(en(nc.semaphore("aXV")))
        aGB = Sem(en(nc.semaphore("aGB")))

        blk = en(nc.Block())

        @blk.sync
        def _(sync):
            for dst, src in [
                (c_id32, ident_f32), (c_idbf, ident_bf), (c_i15, iota15_c),
                (c_o1x128, o_1x128), (c_o1x128b, o_1x128b), (c_o128x1, o_128x1),
                (c_o128bf, o128_bf), (c_o16x1, o_16x1), (c_o1x16, o_1x16),
                (c_u16, u16_cn), (c_iotasp, iota_sp1), (c_qksc, qk_scale),
            ]:
                dS.inc(sync.dma_start(out=dst[:], in_=src.ap()), 16)
            dS.inc(sync.dma_start(out=bqp_s[:], in_=bqp_d), 16)
            dS.inc(sync.dma_start(out=bkp_s[:], in_=bkp_d), 16)
            dS.inc(sync.dma_start(out=bv_s[:], in_=bv_d), 16)
            dS.inc(sync.dma_start(out=gam1[:], in_=gamma_d), 16)
            dS.inc(sync.dma_start(out=role1[:], in_=rolev_d), 16)
            dS.inc(sync.dma_start(out=rolei_s[:], in_=rolei_d), 16)
        LD_CONST = dS.n

        with ExitStack() as ph0:
            p_gb = ph0.enter_context(nc.psum_tensor("p_gb", [128, 1], F32))
            p_eo = ph0.enter_context(nc.psum_tensor("p_eo", [16, 2], F32))

            @blk.vector
            def _(vector):
                vector.wait_ge(dS.h, LD_CONST)
                qv.inc(vector.tensor_scalar(eoff[:, 0:1], role1[:], 2.0 * (CAP + 1), 0.0,
                                            ALU.mult, ALU.add))
                qv.inc(vector.tensor_scalar(eoff[:, 1:2], role1[:], 2.0 * (CAP + 1), float(CAP + 1),
                                            ALU.mult, ALU.add))
                vector.wait_ge(qv.h, qv.n)
                qv.inc(vector.memset(zrow[:], 0.0))
            QV_EOFF = qv.n

            @blk.tensor
            def _(tensor):
                tensor.wait_ge(dS.h, LD_CONST)
                qt.inc(tensor.matmul(p_gb[:], c_o1x128[:], gam1[:], start=True, stop=True))
                tensor.wait_ge(qv.h, QV_EOFF)
                qt.inc(tensor.matmul(p_eo[:], c_o1x16[:], eoff[:], start=True, stop=True))
            QT_GB = qt.n

            @blk.vector
            def _(vector):
                vector.wait_ge(qt.h, QT_GB)
                qv.inc(vector.tensor_copy(gammav[:], p_gb[:]))
                qv.inc(vector.tensor_copy(eo16[:], p_eo[:]))
                qv.inc(vector.tensor_tensor(bqp_sc[:], bqp_s[:], c_qksc[:], ALU.mult))
                qv.inc(vector.tensor_copy(bv_b[:], bv_s[:]))
            QV_PREP = qv.n

        with ExitStack() as ph1:
            r1 = ph1.enter_context
            xall = r1(nc.sbuf_tensor("xall", [128, NT, D], F32))
            xnb = r1(nc.sbuf_tensor("xnb", [128, NT, D], BF16))
            xtb_all = r1(nc.sbuf_tensor("xtb_all", [128, 8, Q], BF16))
            gw_s = r1(nc.sbuf_tensor("gw_s", [128, 8, 8], F32))
            xsq = r1(nc.sbuf_tensor("xsq", [128, D], F32))
            s1v = r1(nc.sbuf_tensor("s1v", [128, 1], F32))
            ssq = r1(nc.sbuf_tensor("ssq", [128, 1], F32))
            meanv = r1(nc.sbuf_tensor("meanv", [128, 1], F32))
            msqv = r1(nc.sbuf_tensor("msqv", [128, 1], F32))
            varv = r1(nc.sbuf_tensor("varv", [128, 1], F32))
            sdv = r1(nc.sbuf_tensor("sdv", [128, 1], F32))
            rstd = r1(nc.sbuf_tensor("rstd", [128, 1], F32))
            xtf = r1(nc.sbuf_tensor("xtf", [128, 2, 128], F32))
            gse = r1(nc.sbuf_tensor("gse", [128, 8], F32))
            gss = r1(nc.sbuf_tensor("gss", [128, 1], F32))
            gsr = r1(nc.sbuf_tensor("gsr", [128, 1], F32))
            prt = r1(nc.sbuf_tensor("prt", [128, 8, 16], F32))
            p_tr = [r1(nc.psum_tensor(f"p_tr{i}", [128, 128], F32)) for i in range(2)]
            p_g = r1(nc.psum_tensor("p_g", [128, 8], F32))

            @blk.sync
            def _(sync):
                sync.wait_ge(dS.h, dS.n)
                dS.inc(sync.dma_start(out=xall[:], in_=x_my.rearrange("(a p) d -> p a d", p=128)), 16)
                dS.inc(sync.dma_start(out=gw_s[:], in_=gwT_d.rearrange("(c p) e -> p c e", p=128)), 16)
            LD_X = dS.n

            @blk.scalar
            def _(scalar):
                scalar.wait_ge(dS.h, LD_X)
                for k in range(NT):
                    sSQ.inc(scalar.activation(xsq[:], xall[:, k, :], AF.Square,
                                              accum_out=ssq[:, 0:1]))
                    scalar.wait_ge(sVAR.h, k + 1)
                    sSQ.inc(scalar.sqrt(sdv[:], varv[:]))
                    if k >= 1:
                        kk = k - 1
                        scalar.wait_ge(sGM.h, 2 * (8 * kk + 7) + 2)
                        if kk > 0:
                            scalar.wait_ge(sPR.h, kk)
                        sGX.inc(scalar.activation(gse[:], p_g[:], AF.Exp,
                                                  accum_out=gss[:, 0:1]))
                kk = NT - 1
                scalar.wait_ge(sGM.h, 2 * (8 * kk + 7) + 2)
                scalar.wait_ge(sPR.h, kk)
                sGX.inc(scalar.activation(gse[:], p_g[:], AF.Exp, accum_out=gss[:, 0:1]))

            def _ln_tile(vector, k):
                vector.wait_ge(sSQ.h, 2 * k)
                qv.inc(vector.reduce_sum(s1v[:], xall[:, k, :], AX.X))
                vector.wait_ge(qv.h, qv.n)
                qv.inc(vector.tensor_scalar(meanv[:], s1v[:], 1.0 / D, None, ALU.mult))
                vector.wait_ge(qv.h, qv.n)
                qv.inc(vector.tensor_mul(msqv[:], meanv[:], meanv[:]))
                vector.wait_ge(sSQ.h, 2 * k + 1)
                qv.inc(vector.tensor_scalar(varv[:], ssq[:], 1.0 / D, 1e-5, ALU.mult, ALU.add))
                vector.wait_ge(qv.h, qv.n)
                sVAR.inc(vector.tensor_sub(varv[:], varv[:], msqv[:]))
                vector.wait_ge(sSQ.h, 2 * k + 2)
                qv.inc(vector.reciprocal(rstd[:], sdv[:]))
                vector.wait_ge(qv.h, qv.n)
                sLN.inc(vector.tensor_scalar(xall[:, k, :], xall[:, k, :], meanv[:, 0:1],
                                             rstd[:, 0:1], ALU.subtract, ALU.mult))
                vector.wait_ge(sLN.h, sLN.n)
                sLN.inc(vector.tensor_copy(xnb[:, k, :], xall[:, k, :]))

            def _evac_tile(vector, k):
                for c in range(8):
                    g = 8 * k + c
                    vector.wait_ge(sGM.h, 2 * g + 1)
                    if g >= 2:
                        vector.wait_ge(sGM.h, 2 * (g - 2) + 2)
                    qv.inc(vector.tensor_copy(xtf[:, g % 2, :], p_tr[g % 2][:]))
                    vector.wait_ge(qv.h, qv.n)
                    sEV.inc(vector.tensor_copy(xtb_all[:, c, 128 * k:128 * (k + 1)],
                                               p_tr[g % 2][:]))

            def _probs_tile(vector, k):
                vector.wait_ge(sGX.h, k + 1)
                qv.inc(vector.reciprocal(gsr[:], gss[:]))
                vector.wait_ge(qv.h, qv.n)
                sPR.inc(vector.tensor_scalar(prt[:, :, k], gse[:], gsr[:, 0:1], None, ALU.mult))

            @blk.vector
            def _(vector):
                vector.wait_ge(dS.h, LD_X)
                for k in range(NT):
                    _ln_tile(vector, k)
                    _evac_tile(vector, k)
                    if k >= 1:
                        _probs_tile(vector, k - 1)
                _probs_tile(vector, NT - 1)
            SLN_END = sLN.n

            @blk.tensor
            def _(tensor):
                for k in range(NT):
                    tensor.wait_ge(sLN.h, 2 * k + 1)
                    if k > 0:
                        tensor.wait_ge(sGX.h, k)
                    for c in range(8):
                        g = 8 * k + c
                        if g >= 2:
                            tensor.wait_ge(sEV.h, g - 1)
                        sGM.inc(tensor.transpose(p_tr[g % 2][:], xall[:, k, 128 * c:128 * (c + 1)],
                                                 c_id32[:]))
                        tensor.wait_ge(sEV.h, g + 1)
                        sGM.inc(tensor.matmul(p_g[:], xtf[:, g % 2, :], gw_s[:, c, :],
                                              start=(c == 0), stop=(c == 7)))
            SGM_END = sGM.n

            SEV_END = sEV.n

            FEED_XNT = []

            @blk.sync
            def _(sync):
                sync.wait_ge(dS.h, dS.n)
                for j in range(4):
                    sync.wait_ge(sEV.h, 8 * (4 * j + 4))
                    for c in range(8):
                        dS.inc(sync.dma_start(out=ag_xnt_i4[j].ap()[c],
                                              in_=xtb_all[:, c, 512 * j:512 * (j + 1)]), 16)
                    sync.wait_ge(dS.h, dS.n)
                    FEED_XNT.append(dS.n)
                sync.wait_ge(sLN.h, SLN_END)
                dS.inc(sync.dma_start(out=xn_stash.ap().rearrange("(a p) d -> p a d", p=128),
                                      in_=xall[:]), 16)
                dS.inc(sync.dma_start(out=ag_xn_i.ap().rearrange("(a p) d -> p a d", p=128),
                                      in_=xnb[:]), 16)
                sync.wait_ge(dS.h, dS.n)
                XN_FEED = dS.n
                sync.wait_ge(sPR.h, NT)
                dS.inc(sync.dma_start(out=pr_stash.ap(), in_=prt[:]), 16)
                for e2 in range(8):
                    dS.inc(sync.dma_start(out=ag_pr_i.ap()[128 * e2:128 * (e2 + 1), :],
                                          in_=prt[:, e2, :]), 16)
                FEED_XNT.append(XN_FEED)
            AG1_FEED = dS.n

            @blk.gpsimd
            def _(gpsimd):
                with allow_ici_all_to_all():
                    for j in range(4):
                        gpsimd.wait_ge(dS.h, FEED_XNT[j])
                        ccS.inc(gpsimd.collective_compute(
                            "AllGather", ALU.bypass, replica_groups=RG,
                            ins=[ag_xnt_i4[j].ap().opt()], outs=[ag_xnt4[j].ap().opt()]))
                    gpsimd.wait_ge(dS.h, FEED_XNT[4])
                    ccS.inc(gpsimd.collective_compute(
                        "AllGather", ALU.bypass, replica_groups=RG,
                        ins=[ag_xn_i.ap().opt()], outs=[ag_xn.ap().opt()]))
                    gpsimd.wait_ge(dS.h, AG1_FEED)
                    ccS.inc(gpsimd.collective_compute(
                        "AllGather", ALU.bypass, replica_groups=RG,
                        ins=[ag_pr_i.ap().opt()], outs=[ag_pr.ap().opt()]))
            AG1_CC = ccS.n
            XNT_CC = AG1_CC - 6

        phW = ExitStack()
        xint = phW.enter_context(nc.sbuf_tensor("xint", [128, 4, 8, 512], BF16))
        xint2 = phW.enter_context(nc.sbuf_tensor("xint2", [128, 2, 8, 512], BF16))
        idx16 = [phW.enter_context(nc.sbuf_tensor(f"idx16_{i}", [128, 132], I16)) for i in range(2)]
        invg16 = phW.enter_context(nc.sbuf_tensor("invg16", [16, 2, 512], I16))

        with ExitStack() as phA:
            rGA = phA.enter_context
            qt2 = rGA(nc.sbuf_tensor("qt2", [128, T], BF16))
            kt2 = rGA(nc.sbuf_tensor("kt2", [128, T], BF16))
            vsb = rGA(nc.sbuf_tensor("vsb", [128, 64, 256], BF16))

            with ExitStack() as ph2:
                r2 = ph2.enter_context
                wq_b = r2(nc.sbuf_tensor("wq_b", [128, 8, 128], BF16))
                wk_b = r2(nc.sbuf_tensor("wk_b", [128, 8, 128], BF16))
                wv_b = r2(nc.sbuf_tensor("wv_b", [128, 8, 256], BF16))
                with ExitStack() as ph2f:
                    wq_f = ph2f.enter_context(nc.sbuf_tensor("wq_f", [128, 8, 128], F32))
                    wk_f = ph2f.enter_context(nc.sbuf_tensor("wk_f", [128, 8, 128], F32))
                    wv_f = ph2f.enter_context(nc.sbuf_tensor("wv_f", [128, 8, 256], F32))

                    @blk.sync
                    def _(sync):
                        sync.wait_ge(dS.h, dS.n)
                        dS.inc(sync.dma_start(out=wq_f[:], in_=wqT_d.rearrange("(c p) o -> p c o", p=128)), 16)
                        dS.inc(sync.dma_start(out=wk_f[:], in_=wkT_d.rearrange("(c p) o -> p c o", p=128)), 16)
                        dS.inc(sync.dma_start(out=wv_f[:], in_=wvT_d.rearrange("(c p) o -> p c o", p=128)), 16)
                    LD_W = dS.n

                    @blk.vector
                    def _(vector):
                        vector.wait_ge(dS.h, LD_W)
                        qv.inc(vector.tensor_copy(wq_b[:], wq_f[:]))
                        qv.inc(vector.tensor_copy(wk_b[:], wk_f[:]))
                        qv.inc(vector.tensor_copy(wv_b[:], wv_f[:]))
                    QV_WC = qv.n

                xbuf = r2(nc.sbuf_tensor("xbuf", [128, 8, Q], BF16))
                p_q2 = [r2(nc.psum_tensor(f"p_q2{i}", [128, 512], F32)) for i in range(2)]
                p_k2 = [r2(nc.psum_tensor(f"p_k2{i}", [128, 512], F32)) for i in range(2)]
                p_v = [r2(nc.psum_tensor(f"p_v{i}", [128, 256], F32)) for i in range(2)]

                @blk.sync
                def _(sync):
                    sync.wait_ge(dS.h, dS.n)
                    sync.wait_ge(qv.h, QV_WC)
                    for r in range(4):
                        if r > 0:
                            sync.wait_ge(sVm.h, 16 * r)
                        for j in range(4):
                            sync.wait_ge(ccS.h, XNT_CC + j + 1)
                            for c in range(8):
                                dS.inc(sync.dma_start(out=xbuf[:, c, 512 * j:512 * (j + 1)],
                                                      in_=ag_xnt4[j].ap()[8 * r + c]), 16)
                            sync.wait_ge(dS.h, dS.n)
                LD_XB_BASE = LD_W

                @blk.tensor
                def _(tensor):
                    tensor.wait_ge(qv.h, QV_WC)
                    for r in range(4):
                        for j in range(4):
                            m = 4 * r + j
                            tensor.wait_ge(dS.h, LD_XB_BASE + 16 * 8 * (4 * r + j + 1))
                            if m >= 2:
                                tensor.wait_ge(sQKe.h, 2 * (m - 1))
                            for c in range(8):
                                inst = tensor.matmul(p_q2[m % 2][:], wq_b[:, c, :],
                                                     xbuf[:, c, 512 * j:512 * (j + 1)],
                                                     start=(c == 0), stop=(c == 7))
                                if c == 7:
                                    sQKm.inc(inst)
                            for c in range(8):
                                inst = tensor.matmul(p_k2[m % 2][:], wk_b[:, c, :],
                                                     xbuf[:, c, 512 * j:512 * (j + 1)],
                                                     start=(c == 0), stop=(c == 7))
                                if c == 7:
                                    sQKm.inc(inst)
                        for tt in range(16):
                            vt = 16 * r + tt
                            if vt >= 2:
                                tensor.wait_ge(sVe.h, vt - 1)
                            for c in range(8):
                                tensor.matmul(p_v[vt % 2][:], xbuf[:, c, 128 * tt:128 * (tt + 1)],
                                              wv_b[:, c, :], start=(c == 0), stop=False)
                            sVm.inc(tensor.matmul(p_v[vt % 2][:], c_o1x128b[:], bv_b[:],
                                                  start=False, stop=True))

                @blk.scalar
                def _(scalar):
                    for r in range(4):
                        for j in range(4):
                            m = 4 * r + j
                            scalar.wait_ge(sQKm.h, 2 * m + 1)
                            sQKe.inc(scalar.activation(qt2[:, 512 * m:512 * (m + 1)], p_q2[m % 2][:],
                                                       AF.Identity, bias=bqp_sc[:, 0:1],
                                                       scale=0.25))
                            scalar.wait_ge(sQKm.h, 2 * m + 2)
                            sQKe.inc(scalar.activation(kt2[:, 512 * m:512 * (m + 1)], p_k2[m % 2][:],
                                                       AF.Identity, bias=bkp_s[:, 0:1]))
                        for tt in range(16):
                            vt = 16 * r + tt
                            scalar.wait_ge(sVm.h, vt + 1)
                            sVe.inc(scalar.activation(vsb[:, vt, :], p_v[vt % 2][:],
                                                      AF.Identity, scale=gammav[:, 0:1]))
                SQKE_END, SVE_END = sQKe.n, sVe.n

            with ExitStack() as ph4a:
                r4 = ph4a.enter_context
                prA = r4(nc.sbuf_tensor("prA", [128, 2, 64], F32))
                prS = r4(nc.sbuf_tensor("prS", [16, 2, 512], F32))
                thv = r4(nc.sbuf_tensor("thv", [1, 15], F32))
                thbc = r4(nc.sbuf_tensor("thbc", [128, 15], F32))
                cmpb = r4(nc.sbuf_tensor("cmpb", [128, 960], F32))
                redb = r4(nc.sbuf_tensor("redb", [128, 15], F32))
                geb = r4(nc.sbuf_tensor("geb", [1, 15], F32))
                jst = r4(nc.sbuf_tensor("jst", [1, 1], F32))
                lob = r4(nc.sbuf_tensor("lob", [1, 2], F32))
                th16 = r4(nc.sbuf_tensor("th16", [16, 1], F32))
                maskb = r4(nc.sbuf_tensor("maskb", [16, 512], F32))
                tot_s = r4(nc.sbuf_tensor("tot_s", [1, 1024], F32))
                tot_b = r4(nc.sbuf_tensor("tot_b", [1, 1024], F32))
                excl = r4(nc.sbuf_tensor("excl", [1, 512], F32))
                slotb = r4(nc.sbuf_tensor("slotb", [16, 512], F32))
                selb = r4(nc.sbuf_tensor("selb", [16, 512], F32))
                invf = r4(nc.sbuf_tensor("invf", [16, 512], F32))
                mif = r4(nc.sbuf_tensor("mif", [16, 512], F32))
                mi16 = r4(nc.sbuf_tensor("mi16", [16, 512], I16))
                nfb = r4(nc.sbuf_tensor("nfb", [1, 1], U32))
                p_thbc = r4(nc.psum_tensor("p_thbc", [128, 15], F32))
                p_cnt = r4(nc.psum_tensor("p_cnt", [1, 15], F32))
                p_slot = r4(nc.psum_tensor("p_slot", [16, 512], F32))
                p_tot = r4(nc.psum_tensor("p_tot", [1, 512], F32))
                p_th16 = r4(nc.psum_tensor("p_th16", [16, 1], F32))

                @blk.sync
                def _(sync):
                    sync.wait_ge(dS.h, dS.n)
                    sync.wait_ge(ccS.h, AG1_CC)
                    sync.wait_ge(sQKe.h, SQKE_END)
                    sync.wait_ge(sVe.h, SVE_END)
                    role_sv = sync.value_load(rolei_s[0:1, 0:1], min_val=0, max_val=3)
                    src_a4 = ag_pr.ap().rearrange("(r e p) c -> e p r c", r=4, e=8, p=128)
                    src_s4 = ag_pr.ap().rearrange("(r e j q) c -> e r q j c", r=4, e=8, j=8, q=16)
                    for x in range(2):
                        esv = role_sv * 2 + x
                        dS.inc(sync.dma_start(
                            out=prA[:, x, :].rearrange("p (i c) -> p i c", i=4),
                            in_=src_a4[bass.ts(esv, 1)].squeeze(0)), 16)
                        srcx = src_s4[bass.ts(esv, 1)].squeeze(0)
                        for r in range(4):
                            dS.inc(sync.dma_start(
                                out=prS[:, x, 128 * r:128 * (r + 1)].rearrange(
                                    "q (j c) -> q j c", j=8),
                                in_=srcx[r]), 16)
                LD_PR = dS.n

                @blk.vector
                def _(vector):
                    vector.wait_ge(dS.h, LD_PR)
                    for x in range(2):
                        RBX = (NSTEP + 4) * x
                        RDX = NSTEP * x
                        if x > 0:
                            vector.wait_ge(rB.h, RBX)
                        qv.inc(vector.memset(lob[:, 0:1], 0.0))
                        vector.wait_ge(qv.h, qv.n)
                        for s in range(NSTEP):
                            delta = 1.0 / (16.0 ** (s + 1))
                            src = lob[:, (s % 2):(s % 2) + 1]
                            dst = lob[:, ((s + 1) % 2):((s + 1) % 2) + 1]
                            rA.inc(vector.tensor_scalar(thv[:], c_i15[:], delta, src,
                                                        ALU.mult, ALU.add))
                            vector.wait_ge(rB.h, RBX + s + 1)
                            qv.inc(vector.tensor_copy(thbc[:], p_thbc[:]))
                            vector.wait_ge(qv.h, qv.n)
                            qv.inc(vector.tensor_tensor(
                                cmpb[:].rearrange("p (a b) -> p a b", a=15),
                                prA[:, x, :].unsqueeze(1).broadcast_to([128, 15, 64]),
                                thbc[:].unsqueeze(2).broadcast_to([128, 15, 64]),
                                ALU.is_ge))
                            vector.wait_ge(qv.h, qv.n)
                            rC.inc(vector.tensor_reduce(redb[:], cmpb[:].rearrange(
                                "p (a b) -> p a b", a=15), AX.X, ALU.add))
                            vector.wait_ge(rD.h, RDX + s + 1)
                            qv.inc(vector.tensor_scalar(geb[:], p_cnt[:], float(CAP), None, ALU.is_ge))
                            vector.wait_ge(qv.h, qv.n)
                            qv.inc(vector.tensor_reduce(jst[:], geb[:], AX.X, ALU.add))
                            vector.wait_ge(qv.h, qv.n)
                            qv.inc(vector.tensor_scalar(dst, jst[:], delta, src, ALU.mult, ALU.add))
                            vector.wait_ge(qv.h, qv.n)
                        rM.inc(vector.engine_nop())
                        vector.wait_ge(rB.h, RBX + NSTEP + 1)
                        qv.inc(vector.tensor_copy(th16[:], p_th16[:]))
                        vector.wait_ge(qv.h, qv.n)
                        rM.inc(vector.tensor_scalar(
                            maskb[:].rearrange("q (r c j) -> q r c j", r=4, c=16),
                            prS[:, x, :].rearrange("q (r j c) -> q r c j", r=4, j=8),
                            th16[:, 0:1], None, ALU.is_ge))
                        vector.wait_ge(rB.h, RBX + NSTEP + 3)
                        qv.inc(vector.tensor_copy(tot_s[:, 512:1024], p_tot[:]))
                        qv.inc(vector.memset(tot_s[:, 0:512], 0.0))
                        qv.inc(vector.memset(tot_b[:, 0:512], 0.0))
                        vector.wait_ge(qv.h, qv.n)
                        bufs = (tot_s, tot_b)
                        for jj in range(9):
                            sh = 1 << jj
                            srcb, dstb = bufs[jj % 2], bufs[(jj + 1) % 2]
                            qv.inc(vector.tensor_add(dstb[:, 512:1024], srcb[:, 512:1024],
                                                     srcb[:, 512 - sh:1024 - sh]))
                            vector.wait_ge(qv.h, qv.n)
                        qv.inc(vector.tensor_copy(excl[:], tot_b[:, 511:1023]))
                        vector.wait_ge(qv.h, qv.n)
                        rM.inc(vector.engine_nop())
                        vector.wait_ge(rB.h, RBX + NSTEP + 4)
                        qv.inc(vector.tensor_copy(slotb[:], p_slot[:]))
                        vector.wait_ge(qv.h, qv.n)
                        qv.inc(vector.tensor_scalar(selb[:], slotb[:], float(CAP - 1), None, ALU.is_le))
                        vector.wait_ge(qv.h, qv.n)
                        qv.inc(vector.tensor_mul(selb[:], selb[:], maskb[:]))
                        vector.wait_ge(qv.h, qv.n)
                        qv.inc(vector.tensor_scalar(invf[:], slotb[:], 1.0, None, ALU.add))
                        vector.wait_ge(qv.h, qv.n)
                        qv.inc(vector.tensor_mul(invf[:], invf[:], selb[:]))
                        vector.wait_ge(qv.h, qv.n)
                        qv.inc(vector.tensor_scalar(invf[:], invf[:], eo16[:, x:x + 1],
                                                    None, ALU.add))
                        vector.wait_ge(qv.h, qv.n)
                        qv.inc(vector.tensor_copy(invg16[:, x, :], invf[:]))
                        vector.wait_ge(qv.h, qv.n)
                        qv.inc(vector.tensor_mul(mif[:], c_iotasp[:], selb[:]))
                        vector.wait_ge(qv.h, qv.n)
                        qv.inc(vector.tensor_scalar(mif[:], mif[:], 1.0, None, ALU.subtract))
                        vector.wait_ge(qv.h, qv.n)
                        if x > 0:
                            vector.wait_ge(gSG.h, x)
                        rM.inc(vector.tensor_copy(mi16[:], mif[:]))
                RM_END = rM.n

                @blk.tensor
                def _(tensor):
                    for x in range(2):
                        RDX = NSTEP * x
                        for s in range(NSTEP):
                            tensor.wait_ge(rA.h, RDX + s + 1)
                            rB.inc(tensor.matmul(p_thbc[:], c_o1x128[:], thv[:],
                                                 start=True, stop=True))
                            tensor.wait_ge(rC.h, RDX + s + 1)
                            rD.inc(tensor.matmul(p_cnt[:], c_o128x1[:], redb[:],
                                                 start=True, stop=True))
                        tensor.wait_ge(rM.h, 4 * x + 1)
                        rB.inc(tensor.matmul(p_th16[:], c_o1x16[:],
                                             lob[:, (NSTEP % 2):(NSTEP % 2) + 1],
                                             start=True, stop=True))
                        tensor.wait_ge(rM.h, 4 * x + 2)
                        rB.inc(tensor.matmul(p_slot[:], c_u16[:], maskb[:],
                                             start=True, stop=False, skip_group_check=True))
                        rB.inc(tensor.matmul(p_tot[:], c_o16x1[:], maskb[:],
                                             start=True, stop=True, skip_group_check=True))
                        tensor.wait_ge(rM.h, 4 * x + 3)
                        rB.inc(tensor.matmul(p_slot[:], c_o1x16[:], excl[:],
                                             start=False, stop=True, skip_group_check=True))

                @blk.gpsimd
                def _(gpsimd):
                    gpsimd.load_library(SG_LIB)
                    for x in range(2):
                        gpsimd.wait_ge(rM.h, 4 * x + 4)
                        gSG.inc(gpsimd.sparse_gather(idx16[x][0:16, 0:132], mi16[:],
                                                     num_found=nfb[:]))

                @blk.sync
                def _(sync):
                    for x in range(2):
                        sync.wait_ge(gSG.h, x + 1)
                        dRI.inc(sync.dma_start(out=idx16[x][16:32, :], in_=idx16[x][0:16, :]), 16)
                        sync.wait_ge(dRI.h, dRI.n)
                        dRI.inc(sync.dma_start(out=idx16[x][32:64, :], in_=idx16[x][0:32, :]), 16)
                        sync.wait_ge(dRI.h, dRI.n)
                        dRI.inc(sync.dma_start(out=idx16[x][64:128, :], in_=idx16[x][0:64, :]), 16)
                    sync.wait_ge(rM.h, 8)
                    for x in range(2):
                        dRI.inc(sync.dma_start(
                            out=ag_inv_i.ap()[16 * x:16 * (x + 1), :],
                            in_=invg16[:, x, :]), 16)
                DRI_END = dRI.n

                @blk.gpsimd
                def _(gpsimd):
                    gpsimd.load_library(MLP_LIB)
                    gpsimd.wait_ge(dRI.h, DRI_END)
                    for h in range(4):
                        dG.inc(gpsimd.dma_gather(
                            xint[:, h], ag_xn.ap(), idx16[0][0:128, 32 * h:32 * (h + 1)],
                            num_idxs=512, num_idxs_reg=512, elem_size=D, transpose=True), 16)
                    ccS.inc(gpsimd.collective_compute(
                        "AllGather", ALU.bypass, replica_groups=RG,
                        ins=[ag_inv_i.ap().opt()], outs=[ag_inv.ap().opt()]))
                XG0_DG = dG.n
                INV_CC = ccS.n

            with ExitStack() as ph3:
                r3 = ph3.enter_context
                eb1 = r3(nc.sbuf_tensor("eb1", [128, 2, 512], BF16))
                eb2 = r3(nc.sbuf_tensor("eb2", [128, 2, 512], BF16))
                nc_bc = r3(nc.sbuf_tensor("nc_bc", [128, 32, 512], BF16))
                e08c = r3(nc.sbuf_tensor("e08c", [128, 2, 512], BF16))
                ber = r3(nc.sbuf_tensor("ber", [128, 4, 512], BF16))
                att_st = r3(nc.sbuf_tensor("att_st", [128, 2, 512], BF16))
                zst = r3(nc.sbuf_tensor("zst", [1, 512], F32))
                ztmp = r3(nc.sbuf_tensor("ztmp", [128, 128], F32))
                ztok = r3(nc.sbuf_tensor("ztok", [128, 128], F32))
                colmax = r3(nc.sbuf_tensor("colmax", [128, 128], F32))
                cm_t = r3(nc.sbuf_tensor("cm_t", [128, 2, 128], F32))
                dqq = r3(nc.sbuf_tensor("dqq", [16, 512], F32))
                srow = r3(nc.sbuf_tensor("srow", [1, 2, 512], F32))
                mxr = r3(nc.sbuf_tensor("mxr", [1, 512], F32))
                lxr = r3(nc.sbuf_tensor("lxr", [1, 512], F32))
                negc = r3(nc.sbuf_tensor("negc", [1, 2, 512], BF16))
                p_sc = [r3(nc.psum_tensor(f"p_sc{i}", [128, 512], F32)) for i in range(4)]
                p_att = r3(nc.psum_tensor("p_att", [128, 512], F32))
                p_z = r3(nc.psum_tensor("p_z", [1, 512], F32))

                RB0, RC0, DRI0 = rB.n, rC.n, dRI.n
                rb_sub = lambda i: RB0 + i + 1 + (1 if i >= 64 else 0)
                rb_tr = lambda h: RB0 + (65 if h == 0 else 130)
                rb_dg = lambda k: RB0 + 131 + k
                rc_red = lambda i: RC0 + (2 + i if i < 64 else 3 + i)
                rc_cme = lambda h: RC0 + (66 if h == 0 else 131)
                rc_dq = lambda k: RC0 + 132 + 3 * k
                rc_mx = lambda k: RC0 + 133 + 3 * k
                rc_negc = lambda k: RC0 + 134 + 3 * k
                RC_PRE_END = rc_negc(31)
                dri_srow = lambda k: DRI0 + 16 * (3 if k == 0 else 2 * k + 2)
                dri_negc = lambda k: DRI0 + 16 * (2 * k + 5 if k < 31 else 66)
                DRI_PRE_END = DRI0 + 16 * 66

                @blk.vector
                def _(vector):
                    vector.wait_ge(sQKe.h, SQKE_END)
                    vector.wait_ge(rB.h, RB0)
                    vector.wait_ge(rM.h, 8)
                    vector.wait_ge(gSG.h, 2)
                    rC.inc(vector.memset(colmax[:, 64:128], 0.0))
                    for h in range(2):
                        for qc in range(64):
                            i = 64 * h + qc
                            vector.wait_ge(rB.h, rb_sub(i))
                            rC.inc(vector.tensor_reduce(
                                colmax[:, qc:qc + 1], p_sc[i % 2][:], AX.X, ALU.max))
                        vector.wait_ge(rB.h, rb_tr(h))
                        rC.inc(vector.tensor_copy(cm_t[:, h, :], p_att[:, 0:128]))
                    for h in range(2):
                        for q16 in range(16):
                            k = 16 * h + q16
                            if k >= 1:
                                vector.wait_ge(rB.h, rb_dg(k - 1))
                            rC.inc(vector.tensor_tensor(
                                dqq[:], qt2[64 * h:64 * h + 16, 512 * q16:512 * (q16 + 1)],
                                kt2[64 * h:64 * h + 16, 512 * q16:512 * (q16 + 1)],
                                ALU.mult))
                            vector.wait_ge(dRI.h, dri_srow(k))
                            vector.wait_ge(rB.h, rb_dg(k))
                            rC.inc(vector.tensor_tensor(mxr[:], p_z[:],
                                                        srow[:, k % 2, :], ALU.max))
                            vector.wait_ge(rC.h, rc_mx(k))
                            qv.inc(vector.scalar_tensor_tensor(
                                lxr[:], mxr[:], 0.2, mxr[:], ALU.mult, ALU.max))
                            vector.wait_ge(qv.h, qv.n)
                            if k >= 2:
                                vector.wait_ge(dRI.h, dri_negc(k - 2))
                                vector.wait_ge(pBC.h, k - 1)
                            rC.inc(vector.tensor_scalar(negc[:, k % 2, :], lxr[:],
                                                        -1.0, -24.0, ALU.mult, ALU.add))

                @blk.tensor
                def _(tensor):
                    tensor.wait_ge(sQKe.h, SQKE_END)
                    tensor.wait_ge(sVe.h, SVE_END)
                    tensor.wait_ge(rM.h, 8)
                    tensor.wait_ge(gSG.h, 2)
                    tensor.wait_ge(dRI.h, DRI0)
                    for h in range(2):
                        for qc in range(64):
                            i = 64 * h + qc
                            if i >= 2:
                                tensor.wait_ge(rC.h, rc_red(i - 2))
                            rB.inc(tensor.matmul(
                                p_sc[i % 2][:],
                                qt2[64 * h:64 * h + 16, 128 * qc:128 * (qc + 1)],
                                kt2[64 * h:64 * h + 16, 0:512],
                                start=True, stop=True))
                        tensor.wait_ge(rC.h, rc_red(64 * (h + 1) - 1))
                        rB.inc(tensor.transpose(p_att[:, 0:128], colmax[:], c_id32[:]))
                    for h in range(2):
                        for q16 in range(16):
                            k = 16 * h + q16
                            tensor.wait_ge(rC.h, rc_dq(k))
                            if k >= 1:
                                tensor.wait_ge(rC.h, rc_mx(k - 1))
                            rB.inc(tensor.matmul(p_z[:], c_o16x1[:], dqq[:],
                                                 start=True, stop=True))

                @blk.sync
                def _(sync):
                    sync.wait_ge(dRI.h, DRI0)
                    sync.wait_ge(sQKe.h, SQKE_END)
                    sync.wait_ge(rB.h, RB0)
                    sync.wait_ge(rM.h, 8)
                    sync.wait_ge(gSG.h, 2)
                    for row in (16, 80):
                        dRI.inc(sync.dma_start(out=kt2[row:row + 1, :],
                                               in_=onesrow.ap()), 16)
                    sync.wait_ge(dRI.h, dRI.n)
                    for h in range(2):
                        for q16 in range(16):
                            k = 16 * h + q16
                            sync.wait_ge(rC.h, rc_cme(h))
                            if k >= 2:
                                sync.wait_ge(rC.h, rc_mx(k - 2))
                            dRI.inc(sync.dma_start(out=srow[0:1, k % 2, :],
                                                   in_=cm_t[4 * q16:4 * q16 + 4, h, :]), 16)
                            sync.wait_ge(dRI.h, dRI.n)
                            if k >= 1:
                                kp = k - 1
                                hp, qp = kp // 16, kp % 16
                                sync.wait_ge(rC.h, rc_negc(kp))
                                dRI.inc(sync.dma_start(
                                    out=qt2[64 * hp + 16:64 * hp + 17,
                                            512 * qp:512 * (qp + 1)],
                                    in_=negc[0:1, kp % 2, :]), 16)
                                sync.wait_ge(dRI.h, dRI.n)
                    sync.wait_ge(rC.h, rc_negc(31))
                    dRI.inc(sync.dma_start(
                        out=qt2[64 + 16:64 + 17, 512 * 15:512 * 16],
                        in_=negc[0:1, 31 % 2, :]), 16)
                    sync.wait_ge(dRI.h, dRI.n)

                NCH = NSB * KC

                @blk.gpsimd
                def _(gpsimd):
                    for s in range(NSB):
                        gpsimd.wait_ge(rC.h, rc_negc(s))
                        pBC.inc(gpsimd.partition_broadcast(
                            nc_bc[0:128, s, :], negc[0:1, s % 2, :], 128))

                @blk.tensor
                def _(tensor):
                    tensor.wait_ge(dRI.h, DRI_PRE_END)
                    tensor.wait_ge(rC.h, RC_PRE_END)

                    def vagg(gp):
                        sbp, jp = divmod(gp, KC)
                        hp = sbp // 16
                        if jp == 0 and sbp >= 1:
                            tensor.wait_ge(aEV.h, 2 * sbp)
                        tensor.matmul(p_att[:],
                                      vsb[:, jp, 128 * hp:128 * (hp + 1)],
                                      ber[:, gp % 4, :],
                                      start=(jp == 0), stop=(jp == KC - 1))
                        pBV.inc(tensor.matmul(p_z[:], c_o128bf[:], ber[:, gp % 4, :],
                                              start=(jp == 0), stop=(jp == KC - 1)))

                    for G in range(0, NCH, 2):
                        sb = G // KC
                        hh, qsb = sb // 16, sb % 16
                        if G >= 4:
                            tensor.wait_ge(pEX.h, G - 2)
                        for g in (G, G + 1):
                            j = g % KC
                            pSC.inc(tensor.matmul(
                                p_sc[g % 4][:],
                                kt2[64 * hh:64 * hh + 17, 128 * j:128 * (j + 1)],
                                qt2[64 * hh:64 * hh + 17, 512 * qsb:512 * (qsb + 1)],
                                start=True, stop=True))
                        if G >= 4:
                            tensor.wait_ge(pLK.h, G - 2)
                            vagg(G - 4)
                            vagg(G - 3)
                    for gp in range(NCH - 4, NCH):
                        tensor.wait_ge(pLK.h, gp + 1)
                        vagg(gp)

                @blk.scalar
                def _(scalar):
                    scalar.wait_ge(dRI.h, DRI_PRE_END)

                    def e08prep(s):
                        scalar.wait_ge(pBC.h, s + 1)
                        if s >= 2:
                            scalar.wait_ge(pLK.h, KC * (s - 1))
                        sE8.inc(scalar.activation(e08c[:, s % 2, :], nc_bc[:, s, :],
                                                  AF.Exp, scale=0.8))

                    e08prep(0)
                    for g in range(NCH):
                        sb, j = divmod(g, KC)
                        if j == 4 and sb < NSB - 1:
                            e08prep(sb + 1)
                        scalar.wait_ge(pSC.h, g + 1)
                        if g >= 2:
                            scalar.wait_ge(pLK.h, g - 1)
                        scalar.activation(eb1[:, g % 2, :], p_sc[g % 4][:], AF.Exp)
                        pEX.inc(scalar.activation(eb2[:, g % 2, :], p_sc[g % 4][:],
                                                  AF.Exp, scale=0.2))

                @blk.vector
                def _(vector):
                    def evac(s):
                        vector.wait_ge(pBV.h, KC * (s + 1))
                        if s >= 2:
                            vector.wait_ge(dAT.h, 16 * (s - 1))
                        aEV.inc(vector.tensor_copy(att_st[:, s % 2, :], p_att[:]))
                        if s >= 1:
                            vector.wait_ge(zTK.h, 16 * s)
                        aEV.inc(vector.tensor_copy(zst[:], p_z[:]))

                    for g in range(NCH):
                        sb, j = divmod(g, KC)
                        if j == 0:
                            vector.wait_ge(sE8.h, sb + 1)
                        if j == 3 and sb >= 1:
                            evac(sb - 1)
                        vector.wait_ge(pEX.h, g + 1)
                        if g >= 4:
                            vector.wait_ge(pBV.h, g - 3)
                        qv.inc(vector.tensor_tensor(
                            eb2[:, g % 2, :], eb2[:, g % 2, :], e08c[:, sb % 2, :],
                            ALU.mult))
                        vector.wait_ge(qv.h, qv.n)
                        pLK.inc(vector.tensor_tensor(
                            ber[:, g % 4, :], eb1[:, g % 2, :], eb2[:, g % 2, :],
                            ALU.max))
                    evac(NSB - 1)
                AEV_ALL = aEV.n

                @blk.sync
                def _(sync):
                    for sb in range(NSB):
                        hh, qsb = sb // 16, sb % 16
                        sync.wait_ge(aEV.h, 2 * sb + 1)
                        dAT.inc(sync.dma_start(
                            out=ag_att_i.ap()[128 * hh:128 * (hh + 1),
                                              512 * qsb:512 * (qsb + 1)],
                            in_=att_st[:, sb % 2, :]), 16)
                        sync.wait_ge(dAT.h, dAT.n)
                        sync.wait_ge(aEV.h, 2 * sb + 2)
                        zTK.inc(sync.dma_start(
                            out=ztmp[64 * hh + 4 * qsb:64 * hh + 4 * qsb + 4, :],
                            in_=zst[0:1, :]), 16)
                        sync.wait_ge(zTK.h, zTK.n)
                DAT_END = dAT.n
                ZTK_DMAS = zTK.n

                @blk.tensor
                def _(tensor):
                    tensor.wait_ge(zTK.h, ZTK_DMAS)
                    zTK.inc(tensor.transpose(p_att[:, 0:128], ztmp[:], c_id32[:]))

                @blk.vector
                def _(vector):
                    vector.wait_ge(zTK.h, ZTK_DMAS + 1)
                    zTK.inc(vector.reciprocal(ztok[:], p_att[:, 0:128]))

                @blk.sync
                def _(sync):
                    sync.wait_ge(dS.h, dS.n)
                    sync.wait_ge(zTK.h, ZTK_DMAS + 2)
                    dst = ag_zi_i.ap().rearrange("(h p) c -> p h c", h=2, p=128)
                    srcz = ztok[:].rearrange("p (h c) -> p h c", h=2)
                    dS.inc(sync.dma_start(out=dst, in_=srcz), 16)
                ZI_FEED = dS.n

                @blk.gpsimd
                def _(gpsimd):
                    gpsimd.wait_ge(dAT.h, DAT_END)
                    gpsimd.wait_ge(dS.h, ZI_FEED)
                    ccS.inc(gpsimd.collective_compute(
                        "AllGather", ALU.bypass, replica_groups=RG,
                        ins=[ag_att_i.ap().opt()], outs=[ag_att.ap().opt()]))
                    ccS.inc(gpsimd.collective_compute(
                        "AllGather", ALU.bypass, replica_groups=RG,
                        ins=[ag_zi_i.ap().opt()], outs=[ag_zi.ap().opt()]))
                ATT_CC = ccS.n

        with ExitStack() as phF:
            rF = phF.enter_context
            w1b = rF(nc.sbuf_tensor("w1b", [128, 8, FF], BF16))
            w2b = rF(nc.sbuf_tensor("w2b", [128, 32, D], BF16))
            hb = rF(nc.sbuf_tensor("hb", [128, 2, 256], BF16))
            ystg = rF(nc.sbuf_tensor("ystg", [128, 2, 2, D], BF16))
            p_h = [rF(nc.psum_tensor(f"p_h{i}", [128, 256], F32)) for i in range(2)]
            p_y = [rF(nc.psum_tensor(f"p_y{i}", [128, 512], F32)) for i in range(4)]

            @blk.gpsimd
            def _(gpsimd):
                for h in range(2):
                    dG.inc(gpsimd.dma_gather(
                        xint2[:, h], ag_xn.ap(), idx16[1][0:128, 32 * h:32 * (h + 1)],
                        num_idxs=512, num_idxs_reg=512, elem_size=D, transpose=True), 16)
                dG.inc(gpsimd.dma_start(out=w1b[:], in_=w1_d[0].rearrange("(c p) f -> p c f", p=128)), 16)
                dG.inc(gpsimd.dma_start(out=w2b[:], in_=w2_d[0].rearrange("(c p) d -> p c d", p=128)), 16)
                gpsimd.wait_ge(fW2.h, 32 * 8)
                dW.inc(gpsimd.dma_start(out=w1b[:], in_=w1_d[1].rearrange("(c p) f -> p c f", p=128)), 16)
                cpS.inc(gpsimd.dma_start(out=w2b[:], in_=w2_d[1].rearrange("(c p) d -> p c d", p=128)), 16)
                for h in range(2, 4):
                    dG2.inc(gpsimd.dma_gather(
                        xint[:, h], ag_xn.ap(), idx16[1][0:128, 32 * h:32 * (h + 1)],
                        num_idxs=512, num_idxs_reg=512, elem_size=D, transpose=True), 16)
            DG_END = dG.n
            W0_READY = DG_END

            @blk.tensor
            def _(tensor):
                for x in range(2):
                    if x == 0:
                        tensor.wait_ge(dG.h, W0_READY)
                    else:
                        tensor.wait_ge(dW.h, 16)

                    def ymm(fc, gtb):
                        u = 32 * gtb + fc
                        if gtb == 8 and fc == 0:
                            tensor.wait_ge(cpS.h, 16)
                        tensor.wait_ge(fH.h, u + 1)
                        if fc == 0:
                            tensor.wait_ge(fY.h, gtb)
                        for tt in range(2):
                            for dh in range(2):
                                inst = tensor.matmul(
                                    p_y[2 * tt + dh][:],
                                    hb[:, fc % 2, 128 * tt:128 * (tt + 1)],
                                    w2b[:, fc, 512 * dh:512 * (dh + 1)],
                                    start=(fc == 0), stop=(fc == 31))
                        fW2.inc(inst)

                    for tb in range(8):
                        gtb = 8 * x + tb
                        xin = xint2 if (x == 1 and tb < 4) else xint
                        if x == 1 and tb == 4:
                            tensor.wait_ge(dG2.h, 32)
                        for fc in range(32):
                            u = 32 * gtb + fc
                            if u >= 2:
                                tensor.wait_ge(fH.h, u - 1)
                            for dc in range(8):
                                inst = tensor.matmul(
                                    p_h[fc % 2][:], w1b[:, dc, 128 * fc:128 * (fc + 1)],
                                    xin[:, tb // 2, dc, 256 * (tb % 2):256 * (tb % 2) + 256],
                                    start=(dc == 0), stop=(dc == 7))
                                if dc == 7:
                                    fHm.inc(inst)
                            if fc >= 1:
                                ymm(fc - 1, gtb)
                        ymm(31, gtb)
                    fDone.inc(tensor.nop())

            @blk.scalar
            def _(scalar):
                for x in range(2):
                    for tb in range(8):
                        gtb = 8 * x + tb
                        for fc in range(32):
                            u = 32 * gtb + fc
                            scalar.wait_ge(fHm.h, u + 1)
                            if u >= 2:
                                scalar.wait_ge(fW2.h, u - 1)
                            fH.inc(scalar.activation(hb[:, fc % 2, :], p_h[fc % 2][:], AF.Gelu))

            @blk.vector
            def _(vector):
                for x in range(2):
                    for tb in range(8):
                        gtb = 8 * x + tb
                        vector.wait_ge(fW2.h, 32 * (gtb + 1))
                        if gtb >= 2:
                            vector.wait_ge(dFF.h, 32 * gtb)
                        for tt in range(2):
                            for dh in range(2):
                                qv.inc(vector.tensor_copy(
                                    ystg[:, gtb % 2, tt, 512 * dh:512 * (dh + 1)],
                                    p_y[2 * tt + dh][:]))
                        vector.wait_ge(qv.h, qv.n)
                        fY.inc(vector.engine_nop())

            @blk.sync
            def _(sync):
                sync.wait_ge(qv.h, QV_EOFF)
                for x in range(2):
                    dFF.inc(sync.dma_start(out=ag_ffn_i.ap()[x * (CAP + 1):x * (CAP + 1) + 1, :],
                                           in_=zrow[:]), 16)
                sync.wait_ge(dFF.h, dFF.n)
                for x in range(2):
                    for tb in range(8):
                        gtb = 8 * x + tb
                        sync.wait_ge(fY.h, gtb + 1)
                        for tt in range(2):
                            row = x * (CAP + 1) + 1 + 256 * tb + 128 * tt
                            dFF.inc(sync.dma_start(out=ag_ffn_i.ap()[row:row + 128, :],
                                                   in_=ystg[:, gtb % 2, tt, :]), 16)
                        sync.wait_ge(dFF.h, dFF.n)
            DFF_END = dFF.n

            @blk.gpsimd
            def _(gpsimd):
                gpsimd.wait_ge(dFF.h, DFF_END)
                ccS.inc(gpsimd.collective_compute(
                    "AllGather", ALU.bypass, replica_groups=RG,
                    ins=[ag_ffn_i.ap().opt()], outs=[ag_ffn.ap().opt()]))
            FFN_CC = ccS.n

        phW.close()

        with ExitStack() as phS:
            rS = phS.enter_context
            acc = rS(nc.sbuf_tensor("acc", [128, NT, D], F32))
            gbuf = rS(nc.sbuf_tensor("gbuf", [128, 2, 16, D], BF16))
            att8 = rS(nc.sbuf_tensor("att8", [128, 2, 8, 512], BF16))
            ziA = rS(nc.sbuf_tensor("ziA", [128, 8, 16], F32))
            g8 = rS(nc.sbuf_tensor("g8", [128, 8, 16], F32))
            idxA = rS(nc.sbuf_tensor("idxA", [128, 8, 128], I16))
            xnr = rS(nc.sbuf_tensor("xnr", [128, 2, D], F32))
            p_at = [rS(nc.psum_tensor(f"p_at{i}", [128, 128], BF16)) for i in range(2)]

            dSC = Sem(en(nc.semaphore("dSC")))

            @blk.sync
            def _(sync):
                sync.wait_ge(fDone.h, 2)
                sync.wait_ge(dFF.h, DFF_END)
                dAS.inc(sync.dma_start(out=acc[:], in_=x_my.rearrange("(a p) d -> p a d", p=128)), 16)
                dAS.inc(sync.dma_start(out=g8[:], in_=pr_stash.ap()), 16)
                sync.wait_ge(dAS.h, dAS.n)
                for k in range(NT):
                    if k >= 1:
                        sync.wait_ge(aXV.h, max(k - 1, 0))
                    dAS.inc(sync.dma_start(
                        out=xnr[:, k % 2, :],
                        in_=xn_stash.ap()[128 * k:128 * (k + 1), :]), 16)
                    sync.wait_ge(dAS.h, dAS.n)

            @blk.scalar
            def _(scalar):
                scalar.wait_ge(fDone.h, 2)
                scalar.wait_ge(dFF.h, DFF_END)
                scalar.wait_ge(ccS.h, ATT_CC)
                scalar.wait_ge(ccS.h, INV_CC)
                role_sv = scalar.value_load(rolei_s[0:1, 0:1], min_val=0, max_val=3)
                src_zi = ag_zi.ap().rearrange("(h p) c -> p h c", h=8, p=128)
                dSC.inc(scalar.dma_start(
                    out=ziA[:], in_=src_zi[:, :, bass.ts(role_sv, 16)]), 16)
                scalar.wait_ge(dSC.h, dSC.n)
                inv_q = ag_inv.ap()[:, bass.ts(role_sv, 128)]
                for e in range(8):
                    dSC.inc(scalar.dma_start(
                        out=idxA[0:16, e, :],
                        in_=inv_q[16 * e:16 * (e + 1)]), 16)
                    scalar.wait_ge(dSC.h, dSC.n)
                    dSC.inc(scalar.dma_start(out=idxA[16:32, e, :], in_=idxA[0:16, e, :]), 16)
                    scalar.wait_ge(dSC.h, dSC.n)
                    dSC.inc(scalar.dma_start(out=idxA[32:64, e, :], in_=idxA[0:32, e, :]), 16)
                    scalar.wait_ge(dSC.h, dSC.n)
                    dSC.inc(scalar.dma_start(out=idxA[64:128, e, :], in_=idxA[0:64, e, :]), 16)
                    scalar.wait_ge(dSC.h, dSC.n)
                src_at = ag_att.ap().rearrange("(h p) t -> p h t", h=8, p=128)
                ap_qat = src_at[:, :, bass.ts(role_sv, 2048)]
                for m in range(4):
                    if m >= 2:
                        scalar.wait_ge(aTR.h, 32 * (m - 1))
                    dSC.inc(scalar.dma_start(
                        out=att8[:, m % 2, :, :],
                        in_=ap_qat[:, :, 512 * m:512 * (m + 1)]), 16)
                    scalar.wait_ge(dSC.h, dSC.n)
            DSC_PRE = 16 + 64 * 8
            DSC_END = dSC.n

            @blk.tensor
            def _(tensor):
                for k in range(NT):
                    tensor.wait_ge(dSC.h, DSC_PRE + 16 * (k // 4 + 1))
                    for h in range(8):
                        g = 8 * k + h
                        if g >= 2:
                            tensor.wait_ge(aAC.h, g - 1)
                        aTR.inc(tensor.transpose(
                            p_at[g % 2][:],
                            att8[:, (k // 4) % 2, h, 128 * (k % 4):128 * (k % 4 + 1)],
                            c_idbf[:]))

            @blk.vector
            def _(vector):
                vector.wait_ge(dAS.h, 32)
                vector.wait_ge(dSC.h, 16)
                for k in range(NT):
                    vector.wait_ge(dAS.h, 32 + 16 * (k + 1))
                    qv.inc(vector.tensor_add(acc[:, k, :], acc[:, k, :], xnr[:, k % 2, :]))
                    vector.wait_ge(qv.h, qv.n)
                    aXV.inc(vector.engine_nop())
                    for h in range(8):
                        g = 8 * k + h
                        vector.wait_ge(aTR.h, g + 1)
                        aAC.inc(vector.scalar_tensor_tensor(
                            acc[:, k, 128 * h:128 * (h + 1)], p_at[g % 2][:],
                            ziA[:, h, k:k + 1], acc[:, k, 128 * h:128 * (h + 1)],
                            ALU.mult, ALU.add))
                        vector.wait_ge(aAC.h, aAC.n)

            aGp = [Sem(en(nc.semaphore("aGp0"))), Sem(en(nc.semaphore("aGp1")))]

            @blk.gpsimd
            def _(gpsimd):
                gpsimd.wait_ge(ccS.h, FFN_CC)
                gpsimd.wait_ge(dSC.h, DSC_PRE)
                for e in range(8):
                    if e >= 2:
                        gpsimd.wait_ge(aGB.h, e - 1)
                        gpsimd.wait_ge(aGp[e % 2].h, 32 * (e // 2))
                    for hf in range(2):
                        aGp[e % 2].inc(gpsimd.dma_gather(
                            gbuf[:, e % 2, 8 * hf:8 * (hf + 1), :], ag_ffn.ap(),
                            idxA[0:128, e, 64 * hf:64 * (hf + 1)],
                            num_idxs=1024, num_idxs_reg=1024, elem_size=D,
                            transpose=False), 16)

            @blk.vector
            def _(vector):
                for e in range(8):
                    vector.wait_ge(aGp[e % 2].h, 32 * (e // 2 + 1))
                    for k in range(NT):
                        qv.inc(vector.scalar_tensor_tensor(
                            acc[:, k, :], gbuf[:, e % 2, k, :],
                            g8[:, e, k:k + 1], acc[:, k, :],
                            ALU.mult, ALU.add))
                        vector.wait_ge(qv.h, qv.n)
                    aGB.inc(vector.engine_nop())

            @blk.sync
            def _(sync):
                sync.wait_ge(aGB.h, 8)
                dO.inc(sync.dma_start(out=out_my.rearrange("(a p) d -> p a d", p=128),
                                      in_=acc[:]), 16)
                sync.wait_ge(dO.h, 16)

    lower_extended_insts(nc)
    return nc


_NC_CACHE = {}


def _get_nc():
    if "nc" not in _NC_CACHE:
        _NC_CACHE["nc"] = build_nc()
    return _NC_CACHE["nc"]


def make_in_maps(L0_attn_x, L0_attn_Wq, L0_attn_bq, L0_attn_Wk, L0_attn_bk,
                 L0_attn_Wv, L0_attn_bv, L0_attn_gamma, L0_ffn_gate_w,
                 L0_ffn_w1, L0_ffn_w2):
    x = np.ascontiguousarray(np.asarray(L0_attn_x, dtype=np.float32))
    Wq = np.asarray(L0_attn_Wq, dtype=np.float32)
    bq = np.asarray(L0_attn_bq, dtype=np.float32)
    Wk = np.asarray(L0_attn_Wk, dtype=np.float32)
    bk = np.asarray(L0_attn_bk, dtype=np.float32)
    Wv = np.asarray(L0_attn_Wv, dtype=np.float32)
    bv = np.asarray(L0_attn_bv, dtype=np.float32)
    gamma = np.asarray(L0_attn_gamma, dtype=np.float32).reshape(1, 1)
    gw = np.asarray(L0_ffn_gate_w, dtype=np.float32)
    w1 = np.asarray(L0_ffn_w1, dtype=np.float32)
    w2 = np.asarray(L0_ffn_w2, dtype=np.float32)

    in_maps = []
    for c in range(8):
        b, role = c // 4, c % 4
        h0, h1 = 2 * role, 2 * role + 1
        wqT = np.zeros((D, 128), np.float32)
        wqT[:, 0:16] = Wq[16 * h0:16 * h0 + 16].T
        wqT[:, 64:80] = Wq[16 * h1:16 * h1 + 16].T
        wkT = np.zeros((D, 128), np.float32)
        wkT[:, 0:16] = Wk[16 * h0:16 * h0 + 16].T
        wkT[:, 64:80] = Wk[16 * h1:16 * h1 + 16].T
        bqp = np.zeros((128, 1), np.float32)
        bqp[0:16, 0] = bq[16 * h0:16 * h0 + 16]
        bqp[64:80, 0] = bq[16 * h1:16 * h1 + 16]
        bkp = np.zeros((128, 1), np.float32)
        bkp[0:16, 0] = bk[16 * h0:16 * h0 + 16]
        bkp[64:80, 0] = bk[16 * h1:16 * h1 + 16]
        in_maps.append({
            "x_my": np.ascontiguousarray(x[b, Q * role:Q * (role + 1), :]),
            "wqT": np.ascontiguousarray(wqT),
            "wkT": np.ascontiguousarray(wkT),
            "bqp": bqp,
            "bkp": bkp,
            "wvT": np.ascontiguousarray(Wv[256 * role:256 * (role + 1), :].T),
            "bv_my": np.ascontiguousarray(bv[256 * role:256 * (role + 1)].reshape(1, 256)),
            "gamma_in": gamma,
            "rolev": np.array([[float(role)]], np.float32),
            "rolei": np.array([[role]], np.int32),
            "gwT": np.ascontiguousarray(gw.T),
            "w1_0": np.ascontiguousarray(w1[h0]),
            "w1_1": np.ascontiguousarray(w1[h1]),
            "w2_0": np.ascontiguousarray(w2[h0]),
            "w2_1": np.ascontiguousarray(w2[h1]),
        })
    return in_maps


def kernel(**inputs):
    in_maps = make_in_maps(**inputs)
    nc = _get_nc()
    res = run_bass_kernel_spmd(nc, in_maps, core_ids=list(range(8)))
    out = np.empty((2, T, D), np.float32)
    for c in range(8):
        b, role = c // 4, c % 4
        out[b, Q * role:Q * (role + 1), :] = np.asarray(res.results[c]["out_my"])
    return out




# version 11
# speedup vs baseline: 1.4063x; 1.0028x over previous
import os
import numpy as np
import ml_dtypes
from contextlib import ExitStack

import concourse.bass as bass
import concourse.mybir as mybir
from concourse.bass_utils import run_bass_kernel_spmd
from concourse.library_overlay import lower_extended_insts
from concourse.library_config import sparse_gather as SG_LIB, mlp as MLP_LIB
from concourse.aot_env import allow_ici_all_to_all

F32 = mybir.dt.float32
BF16 = mybir.dt.bfloat16
F16 = mybir.dt.float16
I16 = mybir.dt.int16
U32 = mybir.dt.uint32
AF = mybir.ActivationFunctionType
ALU = mybir.AluOpType
AX = mybir.AxisListType

T = 8192
D = 1024
CAP = 2048
FF = 4096
Q = 2048
NSTEP = 9
KC = 64
NSB = 32
NT = 16
RG = [[0, 1, 2, 3], [4, 5, 6, 7]]

os.environ.setdefault("TRN_ALLOW_ICI_ALL_TO_ALL", "1")


class Sem:
    def __init__(self, h):
        self.h = h
        self.n = 0

    def inc(self, inst, k=1):
        inst.then_inc(self.h, k)
        self.n += k
        return self.n


def build_nc():
    nc = bass.Bass("TRN2", target_bir_lowering=False, debug=False,
                   detect_race_conditions=True)

    x_my = nc.dram_tensor("x_my", [Q, D], F32, kind="ExternalInput").ap()
    wqT_d = nc.dram_tensor("wqT", [D, 128], F32, kind="ExternalInput").ap()
    wkT_d = nc.dram_tensor("wkT", [D, 128], F32, kind="ExternalInput").ap()
    bqp_d = nc.dram_tensor("bqp", [128, 1], F32, kind="ExternalInput").ap()
    bkp_d = nc.dram_tensor("bkp", [128, 1], F32, kind="ExternalInput").ap()
    wvT_d = nc.dram_tensor("wvT", [D, 256], F32, kind="ExternalInput").ap()
    bv_d = nc.dram_tensor("bv_my", [1, 256], F32, kind="ExternalInput").ap()
    gamma_d = nc.dram_tensor("gamma_in", [1, 1], F32, kind="ExternalInput").ap()
    rolev_d = nc.dram_tensor("rolev", [1, 1], F32, kind="ExternalInput").ap()
    rolei_d = nc.dram_tensor("rolei", [1, 1], mybir.dt.int32, kind="ExternalInput").ap()
    gwT_d = nc.dram_tensor("gwT", [D, 8], F32, kind="ExternalInput").ap()
    w1_d = [nc.dram_tensor(f"w1_{i}", [D, FF], F32, kind="ExternalInput").ap() for i in range(2)]
    w2_d = [nc.dram_tensor(f"w2_{i}", [FF, D], F32, kind="ExternalInput").ap() for i in range(2)]
    out_my = nc.dram_tensor("out_my", [Q, D], F32, kind="ExternalOutput").ap()

    xn_stash = nc.dram_tensor("xn_stash", [Q, D], F32, kind="Internal")
    pr_stash = nc.dram_tensor("pr_stash", [128, 8, 16], F32, kind="Internal")
    ag_xn_i = nc.dram_tensor("ag_xn_i", [Q, D], BF16, kind="Internal")
    ag_xn = nc.dram_tensor("ag_xn", [T, D], BF16, kind="Internal")
    ag_xnt_i4 = [nc.dram_tensor(f"ag_xnt_i{j}", [8, 128, 512], BF16, kind="Internal")
                 for j in range(4)]
    ag_xnt4 = [nc.dram_tensor(f"ag_xnt{j}", [32, 128, 512], BF16, kind="Internal")
               for j in range(4)]
    ag_pr_i = nc.dram_tensor("ag_pr_i", [1024, 16], F32, kind="Internal")
    ag_pr = nc.dram_tensor("ag_pr", [4096, 16], F32, kind="Internal")
    ag_att_i = nc.dram_tensor("ag_att_i", [256, T], BF16, kind="Internal")
    ag_att = nc.dram_tensor("ag_att", [1024, T], BF16, kind="Internal")
    ag_zi_i = nc.dram_tensor("ag_zi_i", [256, 64], F32, kind="Internal")
    ag_zi = nc.dram_tensor("ag_zi", [1024, 64], F32, kind="Internal")
    ag_ffn_i = nc.dram_tensor("ag_ffn_i", [2 * (CAP + 1), D], BF16, kind="Internal")
    ag_ffn = nc.dram_tensor("ag_ffn", [8 * (CAP + 1), D], BF16, kind="Internal")
    ag_inv_i = nc.dram_tensor("ag_inv_i", [32, 512], I16, kind="Internal")
    ag_inv = nc.dram_tensor("ag_inv", [128, 512], I16, kind="Internal")

    ident_f32 = nc.inline_tensor(np.eye(128, dtype=np.float32), "k_ident_f32")
    ident_bf = nc.inline_tensor(np.eye(128, dtype=np.float32).astype(ml_dtypes.bfloat16), "k_ident_bf")
    iota15_c = nc.inline_tensor(np.arange(1, 16, dtype=np.float32).reshape(1, 15), "k_iota15")
    o_1x128 = nc.inline_tensor(np.ones((1, 128), np.float32), "k_o1x128")
    o_1x128b = nc.inline_tensor(np.ones((1, 128), np.float32).astype(ml_dtypes.bfloat16), "k_o1x128b")
    o_128x1 = nc.inline_tensor(np.ones((128, 1), np.float32), "k_o128x1")
    o128_bf = nc.inline_tensor(np.ones((128, 1), np.float32).astype(ml_dtypes.bfloat16), "k_o128bf")
    o_16x1 = nc.inline_tensor(np.ones((16, 1), np.float32), "k_o16x1")
    o_1x16 = nc.inline_tensor(np.ones((1, 16), np.float32), "k_o1x16")
    u16_cn = nc.inline_tensor(np.triu(np.ones((16, 16), np.float32), 1), "k_u16")
    iota_sp1 = nc.inline_tensor(
        (np.arange(T, dtype=np.float32) + 1.0).reshape(512, 16).T.copy(), "k_iotasp")
    scv = np.ones((128, 1), np.float32)
    scv[0:16] = 0.25
    scv[64:80] = 0.25
    qk_scale = nc.inline_tensor(scv, "k_qksc")
    onesrow = nc.inline_tensor(
        np.ones((1, T), np.float32).astype(ml_dtypes.bfloat16), "k_onesrow")

    ctx = ExitStack()
    with ctx:
        en = ctx.enter_context
        c_id32 = en(nc.sbuf_tensor("c_id32", [128, 128], F32))
        c_idbf = en(nc.sbuf_tensor("c_idbf", [128, 128], BF16))
        c_i15 = en(nc.sbuf_tensor("c_i15", [1, 15], F32))
        c_o1x128 = en(nc.sbuf_tensor("c_o1x128", [1, 128], F32))
        c_o1x128b = en(nc.sbuf_tensor("c_o1x128b", [1, 128], BF16))
        c_o128x1 = en(nc.sbuf_tensor("c_o128x1", [128, 1], F32))
        c_o128bf = en(nc.sbuf_tensor("c_o128bf", [128, 1], BF16))
        c_o16x1 = en(nc.sbuf_tensor("c_o16x1", [16, 1], F32))
        c_o1x16 = en(nc.sbuf_tensor("c_o1x16", [1, 16], F32))
        c_u16 = en(nc.sbuf_tensor("c_u16", [16, 16], F32))
        c_iotasp = en(nc.sbuf_tensor("c_iotasp", [16, 512], F32))
        c_qksc = en(nc.sbuf_tensor("c_qksc", [128, 1], F32))
        gammav = en(nc.sbuf_tensor("gammav", [128, 1], F32))
        bqp_s = en(nc.sbuf_tensor("bqp_s", [128, 1], F32))
        bqp_sc = en(nc.sbuf_tensor("bqp_sc", [128, 1], F32))
        bkp_s = en(nc.sbuf_tensor("bkp_s", [128, 1], F32))
        bv_s = en(nc.sbuf_tensor("bv_s", [1, 256], F32))
        bv_b = en(nc.sbuf_tensor("bv_b", [1, 256], BF16))
        gam1 = en(nc.sbuf_tensor("gam1", [1, 1], F32))
        role1 = en(nc.sbuf_tensor("role1", [1, 1], F32))
        rolei_s = en(nc.sbuf_tensor("rolei_s", [1, 1], mybir.dt.int32))
        eoff = en(nc.sbuf_tensor("eoff", [1, 2], F32))
        eo16 = en(nc.sbuf_tensor("eo16", [16, 2], F32))
        zrow = en(nc.sbuf_tensor("zrow", [1, D], BF16))

        qv = Sem(en(nc.semaphore("qv")))
        qs = Sem(en(nc.semaphore("qs")))
        qt = Sem(en(nc.semaphore("qt")))
        dS = Sem(en(nc.semaphore("dS")))
        dG = Sem(en(nc.semaphore("dG")))
        ccS = Sem(en(nc.semaphore("ccS")))
        dO = Sem(en(nc.semaphore("dO")))
        sSQ = Sem(en(nc.semaphore("sSQ")))
        sLN = Sem(en(nc.semaphore("sLN")))
        sEV = Sem(en(nc.semaphore("sEV")))
        sGM = Sem(en(nc.semaphore("sGM")))
        sGX = Sem(en(nc.semaphore("sGX")))
        sPR = Sem(en(nc.semaphore("sPR")))
        sQKm = Sem(en(nc.semaphore("sQKm")))
        sQKe = Sem(en(nc.semaphore("sQKe")))
        sVm = Sem(en(nc.semaphore("sVm")))
        sVe = Sem(en(nc.semaphore("sVe")))
        sXU = Sem(en(nc.semaphore("sXU")))
        pSC = Sem(en(nc.semaphore("pSC")))
        pLK = Sem(en(nc.semaphore("pLK")))
        pEX = Sem(en(nc.semaphore("pEX")))
        pBV = Sem(en(nc.semaphore("pBV")))
        aEV = Sem(en(nc.semaphore("aEV")))
        dAT = Sem(en(nc.semaphore("dAT")))
        rA = Sem(en(nc.semaphore("rA")))
        rB = Sem(en(nc.semaphore("rB")))
        rC = Sem(en(nc.semaphore("rC")))
        rD = Sem(en(nc.semaphore("rD")))
        rM = Sem(en(nc.semaphore("rM")))
        gSG = Sem(en(nc.semaphore("gSG")))
        dRI = Sem(en(nc.semaphore("dRI")))
        fH = Sem(en(nc.semaphore("fH")))
        fHm = Sem(en(nc.semaphore("fHm")))
        fW2 = Sem(en(nc.semaphore("fW2")))
        fY = Sem(en(nc.semaphore("fY")))
        dFF = Sem(en(nc.semaphore("dFF")))
        fDone = Sem(en(nc.semaphore("fDone")))
        zTK = Sem(en(nc.semaphore("zTK")))
        cpS = Sem(en(nc.semaphore("cpS")))
        dW = Sem(en(nc.semaphore("dW")))
        dG2 = Sem(en(nc.semaphore("dG2")))
        sE8 = Sem(en(nc.semaphore("sE8")))
        pBC = Sem(en(nc.semaphore("pBC")))
        dAS = Sem(en(nc.semaphore("dAS")))
        sVAR = Sem(en(nc.semaphore("sVAR")))
        aTR = Sem(en(nc.semaphore("aTR")))
        aAC = Sem(en(nc.semaphore("aAC")))
        aXV = Sem(en(nc.semaphore("aXV")))
        aGB = Sem(en(nc.semaphore("aGB")))

        blk = en(nc.Block())

        @blk.sync
        def _(sync):
            for dst, src in [
                (c_id32, ident_f32), (c_idbf, ident_bf), (c_i15, iota15_c),
                (c_o1x128, o_1x128), (c_o1x128b, o_1x128b), (c_o128x1, o_128x1),
                (c_o128bf, o128_bf), (c_o16x1, o_16x1), (c_o1x16, o_1x16),
                (c_u16, u16_cn), (c_iotasp, iota_sp1), (c_qksc, qk_scale),
            ]:
                dS.inc(sync.dma_start(out=dst[:], in_=src.ap()), 16)
            dS.inc(sync.dma_start(out=bqp_s[:], in_=bqp_d), 16)
            dS.inc(sync.dma_start(out=bkp_s[:], in_=bkp_d), 16)
            dS.inc(sync.dma_start(out=bv_s[:], in_=bv_d), 16)
            dS.inc(sync.dma_start(out=gam1[:], in_=gamma_d), 16)
            dS.inc(sync.dma_start(out=role1[:], in_=rolev_d), 16)
            dS.inc(sync.dma_start(out=rolei_s[:], in_=rolei_d), 16)
        LD_CONST = dS.n

        with ExitStack() as ph0:
            p_gb = ph0.enter_context(nc.psum_tensor("p_gb", [128, 1], F32))
            p_eo = ph0.enter_context(nc.psum_tensor("p_eo", [16, 2], F32))

            @blk.vector
            def _(vector):
                vector.wait_ge(dS.h, LD_CONST)
                qv.inc(vector.tensor_scalar(eoff[:, 0:1], role1[:], 2.0 * (CAP + 1), 0.0,
                                            ALU.mult, ALU.add))
                qv.inc(vector.tensor_scalar(eoff[:, 1:2], role1[:], 2.0 * (CAP + 1), float(CAP + 1),
                                            ALU.mult, ALU.add))
                vector.wait_ge(qv.h, qv.n)
                qv.inc(vector.memset(zrow[:], 0.0))
            QV_EOFF = qv.n

            @blk.tensor
            def _(tensor):
                tensor.wait_ge(dS.h, LD_CONST)
                qt.inc(tensor.matmul(p_gb[:], c_o1x128[:], gam1[:], start=True, stop=True))
                tensor.wait_ge(qv.h, QV_EOFF)
                qt.inc(tensor.matmul(p_eo[:], c_o1x16[:], eoff[:], start=True, stop=True))
            QT_GB = qt.n

            @blk.vector
            def _(vector):
                vector.wait_ge(qt.h, QT_GB)
                qv.inc(vector.tensor_copy(gammav[:], p_gb[:]))
                qv.inc(vector.tensor_copy(eo16[:], p_eo[:]))
                qv.inc(vector.tensor_tensor(bqp_sc[:], bqp_s[:], c_qksc[:], ALU.mult))
                qv.inc(vector.tensor_copy(bv_b[:], bv_s[:]))
            QV_PREP = qv.n

        with ExitStack() as ph1:
            r1 = ph1.enter_context
            xall = r1(nc.sbuf_tensor("xall", [128, NT, D], F32))
            xnb = r1(nc.sbuf_tensor("xnb", [128, NT, D], BF16))
            xtb_all = r1(nc.sbuf_tensor("xtb_all", [128, 8, Q], BF16))
            gw_s = r1(nc.sbuf_tensor("gw_s", [128, 8, 8], F32))
            xsq = r1(nc.sbuf_tensor("xsq", [128, D], F32))
            s1v = r1(nc.sbuf_tensor("s1v", [128, 1], F32))
            ssq = r1(nc.sbuf_tensor("ssq", [128, 1], F32))
            meanv = r1(nc.sbuf_tensor("meanv", [128, 1], F32))
            msqv = r1(nc.sbuf_tensor("msqv", [128, 1], F32))
            varv = r1(nc.sbuf_tensor("varv", [128, 1], F32))
            sdv = r1(nc.sbuf_tensor("sdv", [128, 1], F32))
            rstd = r1(nc.sbuf_tensor("rstd", [128, 1], F32))
            xtf = r1(nc.sbuf_tensor("xtf", [128, 2, 128], F32))
            gse = r1(nc.sbuf_tensor("gse", [128, 8], F32))
            gss = r1(nc.sbuf_tensor("gss", [128, 1], F32))
            gsr = r1(nc.sbuf_tensor("gsr", [128, 1], F32))
            prt = r1(nc.sbuf_tensor("prt", [128, 8, 16], F32))
            p_tr = [r1(nc.psum_tensor(f"p_tr{i}", [128, 128], F32)) for i in range(2)]
            p_g = r1(nc.psum_tensor("p_g", [128, 8], F32))

            @blk.sync
            def _(sync):
                sync.wait_ge(dS.h, dS.n)
                dS.inc(sync.dma_start(out=xall[:], in_=x_my.rearrange("(a p) d -> p a d", p=128)), 16)
                dS.inc(sync.dma_start(out=gw_s[:], in_=gwT_d.rearrange("(c p) e -> p c e", p=128)), 16)
            LD_X = dS.n

            @blk.scalar
            def _(scalar):
                scalar.wait_ge(dS.h, LD_X)
                for k in range(NT):
                    sSQ.inc(scalar.activation(xsq[:], xall[:, k, :], AF.Square,
                                              accum_out=ssq[:, 0:1]))
                    scalar.wait_ge(sVAR.h, k + 1)
                    sSQ.inc(scalar.sqrt(sdv[:], varv[:]))
                    if k >= 1:
                        kk = k - 1
                        scalar.wait_ge(sGM.h, 2 * (8 * kk + 7) + 2)
                        if kk > 0:
                            scalar.wait_ge(sPR.h, kk)
                        sGX.inc(scalar.activation(gse[:], p_g[:], AF.Exp,
                                                  accum_out=gss[:, 0:1]))
                kk = NT - 1
                scalar.wait_ge(sGM.h, 2 * (8 * kk + 7) + 2)
                scalar.wait_ge(sPR.h, kk)
                sGX.inc(scalar.activation(gse[:], p_g[:], AF.Exp, accum_out=gss[:, 0:1]))

            def _ln_tile(vector, k):
                vector.wait_ge(sSQ.h, 2 * k)
                qv.inc(vector.reduce_sum(s1v[:], xall[:, k, :], AX.X))
                vector.wait_ge(qv.h, qv.n)
                qv.inc(vector.tensor_scalar(meanv[:], s1v[:], 1.0 / D, None, ALU.mult))
                vector.wait_ge(qv.h, qv.n)
                qv.inc(vector.tensor_mul(msqv[:], meanv[:], meanv[:]))
                vector.wait_ge(sSQ.h, 2 * k + 1)
                qv.inc(vector.tensor_scalar(varv[:], ssq[:], 1.0 / D, 1e-5, ALU.mult, ALU.add))
                vector.wait_ge(qv.h, qv.n)
                sVAR.inc(vector.tensor_sub(varv[:], varv[:], msqv[:]))
                vector.wait_ge(sSQ.h, 2 * k + 2)
                qv.inc(vector.reciprocal(rstd[:], sdv[:]))
                vector.wait_ge(qv.h, qv.n)
                sLN.inc(vector.tensor_scalar(xall[:, k, :], xall[:, k, :], meanv[:, 0:1],
                                             rstd[:, 0:1], ALU.subtract, ALU.mult))
                vector.wait_ge(sLN.h, sLN.n)
                sLN.inc(vector.tensor_copy(xnb[:, k, :], xall[:, k, :]))

            def _evac_tile(vector, k):
                for c in range(8):
                    g = 8 * k + c
                    vector.wait_ge(sGM.h, 2 * g + 1)
                    if g >= 2:
                        vector.wait_ge(sGM.h, 2 * (g - 2) + 2)
                    qv.inc(vector.tensor_copy(xtf[:, g % 2, :], p_tr[g % 2][:]))
                    vector.wait_ge(qv.h, qv.n)
                    sEV.inc(vector.tensor_copy(xtb_all[:, c, 128 * k:128 * (k + 1)],
                                               p_tr[g % 2][:]))

            def _probs_tile(vector, k):
                vector.wait_ge(sGX.h, k + 1)
                qv.inc(vector.reciprocal(gsr[:], gss[:]))
                vector.wait_ge(qv.h, qv.n)
                sPR.inc(vector.tensor_scalar(prt[:, :, k], gse[:], gsr[:, 0:1], None, ALU.mult))

            @blk.vector
            def _(vector):
                vector.wait_ge(dS.h, LD_X)
                for k in range(NT):
                    _ln_tile(vector, k)
                    _evac_tile(vector, k)
                    if k >= 1:
                        _probs_tile(vector, k - 1)
                _probs_tile(vector, NT - 1)
            SLN_END = sLN.n

            @blk.tensor
            def _(tensor):
                for k in range(NT):
                    tensor.wait_ge(sLN.h, 2 * k + 1)
                    if k > 0:
                        tensor.wait_ge(sGX.h, k)
                    for c in range(8):
                        g = 8 * k + c
                        if g >= 2:
                            tensor.wait_ge(sEV.h, g - 1)
                        sGM.inc(tensor.transpose(p_tr[g % 2][:], xall[:, k, 128 * c:128 * (c + 1)],
                                                 c_id32[:]))
                        tensor.wait_ge(sEV.h, g + 1)
                        sGM.inc(tensor.matmul(p_g[:], xtf[:, g % 2, :], gw_s[:, c, :],
                                              start=(c == 0), stop=(c == 7)))
            SGM_END = sGM.n

            SEV_END = sEV.n

            FEED_XNT = []

            @blk.sync
            def _(sync):
                sync.wait_ge(dS.h, dS.n)
                for j in range(4):
                    sync.wait_ge(sEV.h, 8 * (4 * j + 4))
                    for c in range(8):
                        dS.inc(sync.dma_start(out=ag_xnt_i4[j].ap()[c],
                                              in_=xtb_all[:, c, 512 * j:512 * (j + 1)]), 16)
                    sync.wait_ge(dS.h, dS.n)
                    FEED_XNT.append(dS.n)
                sync.wait_ge(sLN.h, SLN_END)
                dS.inc(sync.dma_start(out=xn_stash.ap().rearrange("(a p) d -> p a d", p=128),
                                      in_=xall[:]), 16)
                dS.inc(sync.dma_start(out=ag_xn_i.ap().rearrange("(a p) d -> p a d", p=128),
                                      in_=xnb[:]), 16)
                sync.wait_ge(dS.h, dS.n)
                XN_FEED = dS.n
                sync.wait_ge(sPR.h, NT)
                dS.inc(sync.dma_start(out=pr_stash.ap(), in_=prt[:]), 16)
                for e2 in range(8):
                    dS.inc(sync.dma_start(out=ag_pr_i.ap()[128 * e2:128 * (e2 + 1), :],
                                          in_=prt[:, e2, :]), 16)
                FEED_XNT.append(XN_FEED)
            AG1_FEED = dS.n

            @blk.gpsimd
            def _(gpsimd):
                with allow_ici_all_to_all():
                    for j in range(4):
                        gpsimd.wait_ge(dS.h, FEED_XNT[j])
                        ccS.inc(gpsimd.collective_compute(
                            "AllGather", ALU.bypass, replica_groups=RG,
                            ins=[ag_xnt_i4[j].ap().opt()], outs=[ag_xnt4[j].ap().opt()]))
                    gpsimd.wait_ge(dS.h, FEED_XNT[4])
                    ccS.inc(gpsimd.collective_compute(
                        "AllGather", ALU.bypass, replica_groups=RG,
                        ins=[ag_xn_i.ap().opt()], outs=[ag_xn.ap().opt()]))
                    gpsimd.wait_ge(dS.h, AG1_FEED)
                    ccS.inc(gpsimd.collective_compute(
                        "AllGather", ALU.bypass, replica_groups=RG,
                        ins=[ag_pr_i.ap().opt()], outs=[ag_pr.ap().opt()]))
            AG1_CC = ccS.n
            XNT_CC = AG1_CC - 6

        phW = ExitStack()
        xint = phW.enter_context(nc.sbuf_tensor("xint", [128, 4, 8, 512], BF16))
        xint2 = phW.enter_context(nc.sbuf_tensor("xint2", [128, 2, 8, 512], BF16))
        idx16 = [phW.enter_context(nc.sbuf_tensor(f"idx16_{i}", [128, 132], I16)) for i in range(2)]
        invg16 = phW.enter_context(nc.sbuf_tensor("invg16", [16, 2, 512], I16))

        with ExitStack() as phA:
            rGA = phA.enter_context
            qt2 = rGA(nc.sbuf_tensor("qt2", [128, T], BF16))
            kt2 = rGA(nc.sbuf_tensor("kt2", [128, T], BF16))
            vsb = rGA(nc.sbuf_tensor("vsb", [128, 64, 256], BF16))

            with ExitStack() as ph2:
                r2 = ph2.enter_context
                wq_b = r2(nc.sbuf_tensor("wq_b", [128, 8, 128], BF16))
                wk_b = r2(nc.sbuf_tensor("wk_b", [128, 8, 128], BF16))
                wv_b = r2(nc.sbuf_tensor("wv_b", [128, 8, 256], BF16))
                with ExitStack() as ph2f:
                    wq_f = ph2f.enter_context(nc.sbuf_tensor("wq_f", [128, 8, 128], F32))
                    wk_f = ph2f.enter_context(nc.sbuf_tensor("wk_f", [128, 8, 128], F32))
                    wv_f = ph2f.enter_context(nc.sbuf_tensor("wv_f", [128, 8, 256], F32))

                    @blk.sync
                    def _(sync):
                        sync.wait_ge(dS.h, dS.n)
                        dS.inc(sync.dma_start(out=wq_f[:], in_=wqT_d.rearrange("(c p) o -> p c o", p=128)), 16)
                        dS.inc(sync.dma_start(out=wk_f[:], in_=wkT_d.rearrange("(c p) o -> p c o", p=128)), 16)
                        dS.inc(sync.dma_start(out=wv_f[:], in_=wvT_d.rearrange("(c p) o -> p c o", p=128)), 16)
                    LD_W = dS.n

                    @blk.vector
                    def _(vector):
                        vector.wait_ge(dS.h, LD_W)
                        qv.inc(vector.tensor_copy(wq_b[:], wq_f[:]))
                        qv.inc(vector.tensor_copy(wk_b[:], wk_f[:]))
                        qv.inc(vector.tensor_copy(wv_b[:], wv_f[:]))
                    QV_WC = qv.n

                xbuf = r2(nc.sbuf_tensor("xbuf", [128, 8, Q], BF16))
                p_q2 = [r2(nc.psum_tensor(f"p_q2{i}", [128, 512], F32)) for i in range(2)]
                p_k2 = [r2(nc.psum_tensor(f"p_k2{i}", [128, 512], F32)) for i in range(2)]
                p_v = [r2(nc.psum_tensor(f"p_v{i}", [128, 256], F32)) for i in range(2)]

                @blk.sync
                def _(sync):
                    sync.wait_ge(dS.h, dS.n)
                    sync.wait_ge(qv.h, QV_WC)
                    for r in range(4):
                        if r > 0:
                            sync.wait_ge(sVm.h, 16 * r)
                        for j in range(4):
                            sync.wait_ge(ccS.h, XNT_CC + j + 1)
                            for c in range(8):
                                dS.inc(sync.dma_start(out=xbuf[:, c, 512 * j:512 * (j + 1)],
                                                      in_=ag_xnt4[j].ap()[8 * r + c]), 16)
                            sync.wait_ge(dS.h, dS.n)
                LD_XB_BASE = LD_W

                @blk.tensor
                def _(tensor):
                    tensor.wait_ge(qv.h, QV_WC)
                    for r in range(4):
                        for j in range(4):
                            m = 4 * r + j
                            tensor.wait_ge(dS.h, LD_XB_BASE + 16 * 8 * (4 * r + j + 1))
                            if m >= 2:
                                tensor.wait_ge(sQKe.h, 2 * (m - 1))
                            for c in range(8):
                                inst = tensor.matmul(p_q2[m % 2][:], wq_b[:, c, :],
                                                     xbuf[:, c, 512 * j:512 * (j + 1)],
                                                     start=(c == 0), stop=(c == 7))
                                if c == 7:
                                    sQKm.inc(inst)
                            for c in range(8):
                                inst = tensor.matmul(p_k2[m % 2][:], wk_b[:, c, :],
                                                     xbuf[:, c, 512 * j:512 * (j + 1)],
                                                     start=(c == 0), stop=(c == 7))
                                if c == 7:
                                    sQKm.inc(inst)
                        for tt in range(16):
                            vt = 16 * r + tt
                            if vt >= 2:
                                tensor.wait_ge(sVe.h, vt - 1)
                            for c in range(8):
                                tensor.matmul(p_v[vt % 2][:], xbuf[:, c, 128 * tt:128 * (tt + 1)],
                                              wv_b[:, c, :], start=(c == 0), stop=False)
                            sVm.inc(tensor.matmul(p_v[vt % 2][:], c_o1x128b[:], bv_b[:],
                                                  start=False, stop=True))

                @blk.scalar
                def _(scalar):
                    for r in range(4):
                        for j in range(4):
                            m = 4 * r + j
                            scalar.wait_ge(sQKm.h, 2 * m + 1)
                            sQKe.inc(scalar.activation(qt2[:, 512 * m:512 * (m + 1)], p_q2[m % 2][:],
                                                       AF.Identity, bias=bqp_sc[:, 0:1],
                                                       scale=0.25))
                            scalar.wait_ge(sQKm.h, 2 * m + 2)
                            sQKe.inc(scalar.activation(kt2[:, 512 * m:512 * (m + 1)], p_k2[m % 2][:],
                                                       AF.Identity, bias=bkp_s[:, 0:1]))
                        for tt in range(16):
                            vt = 16 * r + tt
                            scalar.wait_ge(sVm.h, vt + 1)
                            sVe.inc(scalar.activation(vsb[:, vt, :], p_v[vt % 2][:],
                                                      AF.Identity, scale=gammav[:, 0:1]))
                SQKE_END, SVE_END = sQKe.n, sVe.n

            with ExitStack() as ph4a:
                r4 = ph4a.enter_context
                prA = r4(nc.sbuf_tensor("prA", [128, 2, 64], F32))
                prS = r4(nc.sbuf_tensor("prS", [16, 2, 512], F32))
                thv = r4(nc.sbuf_tensor("thv", [1, 15], F32))
                thbc = r4(nc.sbuf_tensor("thbc", [128, 15], F32))
                cmpb = r4(nc.sbuf_tensor("cmpb", [128, 960], F32))
                redb = r4(nc.sbuf_tensor("redb", [128, 15], F32))
                geb = r4(nc.sbuf_tensor("geb", [1, 15], F32))
                jst = r4(nc.sbuf_tensor("jst", [1, 1], F32))
                lob = r4(nc.sbuf_tensor("lob", [1, 2], F32))
                th16 = r4(nc.sbuf_tensor("th16", [16, 1], F32))
                maskb = r4(nc.sbuf_tensor("maskb", [16, 512], F32))
                tot_s = r4(nc.sbuf_tensor("tot_s", [1, 1024], F32))
                tot_b = r4(nc.sbuf_tensor("tot_b", [1, 1024], F32))
                excl = r4(nc.sbuf_tensor("excl", [1, 512], F32))
                slotb = r4(nc.sbuf_tensor("slotb", [16, 512], F32))
                selb = r4(nc.sbuf_tensor("selb", [16, 512], F32))
                invf = r4(nc.sbuf_tensor("invf", [16, 512], F32))
                mif = r4(nc.sbuf_tensor("mif", [16, 512], F32))
                mi16 = r4(nc.sbuf_tensor("mi16", [16, 512], I16))
                nfb = r4(nc.sbuf_tensor("nfb", [1, 1], U32))
                p_thbc = r4(nc.psum_tensor("p_thbc", [128, 15], F32))
                p_cnt = r4(nc.psum_tensor("p_cnt", [1, 15], F32))
                p_slot = r4(nc.psum_tensor("p_slot", [16, 512], F32))
                p_tot = r4(nc.psum_tensor("p_tot", [1, 512], F32))
                p_th16 = r4(nc.psum_tensor("p_th16", [16, 1], F32))

                @blk.sync
                def _(sync):
                    sync.wait_ge(dS.h, dS.n)
                    sync.wait_ge(ccS.h, AG1_CC)
                    sync.wait_ge(sQKe.h, SQKE_END)
                    sync.wait_ge(sVe.h, SVE_END)
                    role_sv = sync.value_load(rolei_s[0:1, 0:1], min_val=0, max_val=3)
                    src_a4 = ag_pr.ap().rearrange("(r e p) c -> e p r c", r=4, e=8, p=128)
                    src_s4 = ag_pr.ap().rearrange("(r e j q) c -> e r q j c", r=4, e=8, j=8, q=16)
                    for x in range(2):
                        esv = role_sv * 2 + x
                        dS.inc(sync.dma_start(
                            out=prA[:, x, :].rearrange("p (i c) -> p i c", i=4),
                            in_=src_a4[bass.ts(esv, 1)].squeeze(0)), 16)
                        srcx = src_s4[bass.ts(esv, 1)].squeeze(0)
                        for r in range(4):
                            dS.inc(sync.dma_start(
                                out=prS[:, x, 128 * r:128 * (r + 1)].rearrange(
                                    "q (j c) -> q j c", j=8),
                                in_=srcx[r]), 16)
                LD_PR = dS.n

                @blk.vector
                def _(vector):
                    vector.wait_ge(dS.h, LD_PR)
                    for x in range(2):
                        RBX = (NSTEP + 4) * x
                        RDX = NSTEP * x
                        if x > 0:
                            vector.wait_ge(rB.h, RBX)
                        qv.inc(vector.memset(lob[:, 0:1], 0.0))
                        vector.wait_ge(qv.h, qv.n)
                        for s in range(NSTEP):
                            delta = 1.0 / (16.0 ** (s + 1))
                            src = lob[:, (s % 2):(s % 2) + 1]
                            dst = lob[:, ((s + 1) % 2):((s + 1) % 2) + 1]
                            rA.inc(vector.tensor_scalar(thv[:], c_i15[:], delta, src,
                                                        ALU.mult, ALU.add))
                            vector.wait_ge(rB.h, RBX + s + 1)
                            qv.inc(vector.tensor_copy(thbc[:], p_thbc[:]))
                            vector.wait_ge(qv.h, qv.n)
                            qv.inc(vector.tensor_tensor(
                                cmpb[:].rearrange("p (a b) -> p a b", a=15),
                                prA[:, x, :].unsqueeze(1).broadcast_to([128, 15, 64]),
                                thbc[:].unsqueeze(2).broadcast_to([128, 15, 64]),
                                ALU.is_ge))
                            vector.wait_ge(qv.h, qv.n)
                            rC.inc(vector.tensor_reduce(redb[:], cmpb[:].rearrange(
                                "p (a b) -> p a b", a=15), AX.X, ALU.add))
                            vector.wait_ge(rD.h, RDX + s + 1)
                            qv.inc(vector.tensor_scalar(geb[:], p_cnt[:], float(CAP), None, ALU.is_ge))
                            vector.wait_ge(qv.h, qv.n)
                            qv.inc(vector.tensor_reduce(jst[:], geb[:], AX.X, ALU.add))
                            vector.wait_ge(qv.h, qv.n)
                            qv.inc(vector.tensor_scalar(dst, jst[:], delta, src, ALU.mult, ALU.add))
                            vector.wait_ge(qv.h, qv.n)
                        rM.inc(vector.engine_nop())
                        vector.wait_ge(rB.h, RBX + NSTEP + 1)
                        qv.inc(vector.tensor_copy(th16[:], p_th16[:]))
                        vector.wait_ge(qv.h, qv.n)
                        rM.inc(vector.tensor_scalar(
                            maskb[:].rearrange("q (r c j) -> q r c j", r=4, c=16),
                            prS[:, x, :].rearrange("q (r j c) -> q r c j", r=4, j=8),
                            th16[:, 0:1], None, ALU.is_ge))
                        vector.wait_ge(rB.h, RBX + NSTEP + 3)
                        qv.inc(vector.tensor_copy(tot_s[:, 512:1024], p_tot[:]))
                        qv.inc(vector.memset(tot_s[:, 0:512], 0.0))
                        qv.inc(vector.memset(tot_b[:, 0:512], 0.0))
                        vector.wait_ge(qv.h, qv.n)
                        bufs = (tot_s, tot_b)
                        for jj in range(9):
                            sh = 1 << jj
                            srcb, dstb = bufs[jj % 2], bufs[(jj + 1) % 2]
                            qv.inc(vector.tensor_add(dstb[:, 512:1024], srcb[:, 512:1024],
                                                     srcb[:, 512 - sh:1024 - sh]))
                            vector.wait_ge(qv.h, qv.n)
                        qv.inc(vector.tensor_copy(excl[:], tot_b[:, 511:1023]))
                        vector.wait_ge(qv.h, qv.n)
                        rM.inc(vector.engine_nop())
                        vector.wait_ge(rB.h, RBX + NSTEP + 4)
                        qv.inc(vector.tensor_copy(slotb[:], p_slot[:]))
                        vector.wait_ge(qv.h, qv.n)
                        qv.inc(vector.tensor_scalar(selb[:], slotb[:], float(CAP - 1), None, ALU.is_le))
                        vector.wait_ge(qv.h, qv.n)
                        qv.inc(vector.tensor_mul(selb[:], selb[:], maskb[:]))
                        vector.wait_ge(qv.h, qv.n)
                        qv.inc(vector.tensor_scalar(invf[:], slotb[:], 1.0, None, ALU.add))
                        vector.wait_ge(qv.h, qv.n)
                        qv.inc(vector.tensor_mul(invf[:], invf[:], selb[:]))
                        vector.wait_ge(qv.h, qv.n)
                        qv.inc(vector.tensor_scalar(invf[:], invf[:], eo16[:, x:x + 1],
                                                    None, ALU.add))
                        vector.wait_ge(qv.h, qv.n)
                        qv.inc(vector.tensor_copy(invg16[:, x, :], invf[:]))
                        vector.wait_ge(qv.h, qv.n)
                        qv.inc(vector.tensor_mul(mif[:], c_iotasp[:], selb[:]))
                        vector.wait_ge(qv.h, qv.n)
                        qv.inc(vector.tensor_scalar(mif[:], mif[:], 1.0, None, ALU.subtract))
                        vector.wait_ge(qv.h, qv.n)
                        if x > 0:
                            vector.wait_ge(gSG.h, x)
                        rM.inc(vector.tensor_copy(mi16[:], mif[:]))
                RM_END = rM.n

                @blk.tensor
                def _(tensor):
                    for x in range(2):
                        RDX = NSTEP * x
                        for s in range(NSTEP):
                            tensor.wait_ge(rA.h, RDX + s + 1)
                            rB.inc(tensor.matmul(p_thbc[:], c_o1x128[:], thv[:],
                                                 start=True, stop=True))
                            tensor.wait_ge(rC.h, RDX + s + 1)
                            rD.inc(tensor.matmul(p_cnt[:], c_o128x1[:], redb[:],
                                                 start=True, stop=True))
                        tensor.wait_ge(rM.h, 4 * x + 1)
                        rB.inc(tensor.matmul(p_th16[:], c_o1x16[:],
                                             lob[:, (NSTEP % 2):(NSTEP % 2) + 1],
                                             start=True, stop=True))
                        tensor.wait_ge(rM.h, 4 * x + 2)
                        rB.inc(tensor.matmul(p_slot[:], c_u16[:], maskb[:],
                                             start=True, stop=False, skip_group_check=True))
                        rB.inc(tensor.matmul(p_tot[:], c_o16x1[:], maskb[:],
                                             start=True, stop=True, skip_group_check=True))
                        tensor.wait_ge(rM.h, 4 * x + 3)
                        rB.inc(tensor.matmul(p_slot[:], c_o1x16[:], excl[:],
                                             start=False, stop=True, skip_group_check=True))

                @blk.gpsimd
                def _(gpsimd):
                    gpsimd.load_library(SG_LIB)
                    for x in range(2):
                        gpsimd.wait_ge(rM.h, 4 * x + 4)
                        gSG.inc(gpsimd.sparse_gather(idx16[x][0:16, 0:132], mi16[:],
                                                     num_found=nfb[:]))

                @blk.sync
                def _(sync):
                    for x in range(2):
                        sync.wait_ge(gSG.h, x + 1)
                        dRI.inc(sync.dma_start(out=idx16[x][16:32, :], in_=idx16[x][0:16, :]), 16)
                        sync.wait_ge(dRI.h, dRI.n)
                        dRI.inc(sync.dma_start(out=idx16[x][32:64, :], in_=idx16[x][0:32, :]), 16)
                        sync.wait_ge(dRI.h, dRI.n)
                        dRI.inc(sync.dma_start(out=idx16[x][64:128, :], in_=idx16[x][0:64, :]), 16)
                    sync.wait_ge(rM.h, 8)
                    for x in range(2):
                        dRI.inc(sync.dma_start(
                            out=ag_inv_i.ap()[16 * x:16 * (x + 1), :],
                            in_=invg16[:, x, :]), 16)
                DRI_END = dRI.n

                @blk.gpsimd
                def _(gpsimd):
                    gpsimd.load_library(MLP_LIB)
                    gpsimd.wait_ge(dRI.h, DRI_END)
                    for h in range(4):
                        dG.inc(gpsimd.dma_gather(
                            xint[:, h], ag_xn.ap(), idx16[0][0:128, 32 * h:32 * (h + 1)],
                            num_idxs=512, num_idxs_reg=512, elem_size=D, transpose=True), 16)
                    ccS.inc(gpsimd.collective_compute(
                        "AllGather", ALU.bypass, replica_groups=RG,
                        ins=[ag_inv_i.ap().opt()], outs=[ag_inv.ap().opt()]))
                XG0_DG = dG.n
                INV_CC = ccS.n

            with ExitStack() as ph3:
                r3 = ph3.enter_context
                eb1 = r3(nc.sbuf_tensor("eb1", [128, 2, 512], BF16))
                eb2 = r3(nc.sbuf_tensor("eb2", [128, 2, 512], BF16))
                nc_bc = r3(nc.sbuf_tensor("nc_bc", [128, 32, 512], BF16))
                e08c = r3(nc.sbuf_tensor("e08c", [128, 2, 512], BF16))
                ber = r3(nc.sbuf_tensor("ber", [128, 4, 512], BF16))
                att_st = r3(nc.sbuf_tensor("att_st", [128, 2, 512], BF16))
                zst = r3(nc.sbuf_tensor("zst", [1, 512], F32))
                ztmp = r3(nc.sbuf_tensor("ztmp", [128, 128], F32))
                ztok = r3(nc.sbuf_tensor("ztok", [128, 128], F32))
                colmax = r3(nc.sbuf_tensor("colmax", [128, 128], F32))
                cm_t = r3(nc.sbuf_tensor("cm_t", [128, 2, 128], F32))
                dqq = r3(nc.sbuf_tensor("dqq", [16, 512], F32))
                srow = r3(nc.sbuf_tensor("srow", [1, 2, 512], F32))
                mxr = r3(nc.sbuf_tensor("mxr", [1, 512], F32))
                lxr = r3(nc.sbuf_tensor("lxr", [1, 512], F32))
                negc = r3(nc.sbuf_tensor("negc", [1, 2, 512], BF16))
                p_sc = [r3(nc.psum_tensor(f"p_sc{i}", [128, 512], F32)) for i in range(4)]
                p_att = [r3(nc.psum_tensor(f"p_att{i}", [128, 512], F32)) for i in range(2)]
                p_z = [r3(nc.psum_tensor(f"p_z{i}", [1, 512], F32)) for i in range(2)]

                RB0, RC0, DRI0 = rB.n, rC.n, dRI.n
                rb_sub = lambda i: RB0 + i + 1 + (1 if i >= 64 else 0)
                rb_tr = lambda h: RB0 + (65 if h == 0 else 130)
                rb_dg = lambda k: RB0 + 131 + k
                rc_red = lambda i: RC0 + (2 + i if i < 64 else 3 + i)
                rc_cme = lambda h: RC0 + (66 if h == 0 else 131)
                rc_dq = lambda k: RC0 + 132 + 3 * k
                rc_mx = lambda k: RC0 + 133 + 3 * k
                rc_negc = lambda k: RC0 + 134 + 3 * k
                RC_PRE_END = rc_negc(31)
                dri_srow = lambda k: DRI0 + 16 * (3 if k == 0 else 2 * k + 2)
                dri_negc = lambda k: DRI0 + 16 * (2 * k + 5 if k < 31 else 66)
                DRI_PRE_END = DRI0 + 16 * 66

                @blk.vector
                def _(vector):
                    vector.wait_ge(sQKe.h, SQKE_END)
                    vector.wait_ge(rB.h, RB0)
                    vector.wait_ge(rM.h, 8)
                    vector.wait_ge(gSG.h, 2)
                    rC.inc(vector.memset(colmax[:, 64:128], 0.0))
                    for h in range(2):
                        for qc in range(64):
                            i = 64 * h + qc
                            vector.wait_ge(rB.h, rb_sub(i))
                            rC.inc(vector.tensor_reduce(
                                colmax[:, qc:qc + 1], p_sc[i % 2][:], AX.X, ALU.max))
                        vector.wait_ge(rB.h, rb_tr(h))
                        rC.inc(vector.tensor_copy(cm_t[:, h, :], p_att[0][:, 0:128]))
                    for h in range(2):
                        for q16 in range(16):
                            k = 16 * h + q16
                            if k >= 1:
                                vector.wait_ge(rB.h, rb_dg(k - 1))
                            rC.inc(vector.tensor_tensor(
                                dqq[:], qt2[64 * h:64 * h + 16, 512 * q16:512 * (q16 + 1)],
                                kt2[64 * h:64 * h + 16, 512 * q16:512 * (q16 + 1)],
                                ALU.mult))
                            vector.wait_ge(dRI.h, dri_srow(k))
                            vector.wait_ge(rB.h, rb_dg(k))
                            rC.inc(vector.tensor_tensor(mxr[:], p_z[0][:],
                                                        srow[:, k % 2, :], ALU.max))
                            vector.wait_ge(rC.h, rc_mx(k))
                            qv.inc(vector.scalar_tensor_tensor(
                                lxr[:], mxr[:], 0.2, mxr[:], ALU.mult, ALU.max))
                            vector.wait_ge(qv.h, qv.n)
                            if k >= 2:
                                vector.wait_ge(dRI.h, dri_negc(k - 2))
                                vector.wait_ge(pBC.h, k - 1)
                            rC.inc(vector.tensor_scalar(negc[:, k % 2, :], lxr[:],
                                                        -1.0, -24.0, ALU.mult, ALU.add))

                @blk.tensor
                def _(tensor):
                    tensor.wait_ge(sQKe.h, SQKE_END)
                    tensor.wait_ge(sVe.h, SVE_END)
                    tensor.wait_ge(rM.h, 8)
                    tensor.wait_ge(gSG.h, 2)
                    tensor.wait_ge(dRI.h, DRI0)
                    for h in range(2):
                        for qc in range(64):
                            i = 64 * h + qc
                            if i >= 2:
                                tensor.wait_ge(rC.h, rc_red(i - 2))
                            rB.inc(tensor.matmul(
                                p_sc[i % 2][:],
                                qt2[64 * h:64 * h + 16, 128 * qc:128 * (qc + 1)],
                                kt2[64 * h:64 * h + 16, 0:512],
                                start=True, stop=True))
                        tensor.wait_ge(rC.h, rc_red(64 * (h + 1) - 1))
                        rB.inc(tensor.transpose(p_att[0][:, 0:128], colmax[:], c_id32[:]))
                    for h in range(2):
                        for q16 in range(16):
                            k = 16 * h + q16
                            tensor.wait_ge(rC.h, rc_dq(k))
                            if k >= 1:
                                tensor.wait_ge(rC.h, rc_mx(k - 1))
                            rB.inc(tensor.matmul(p_z[0][:], c_o16x1[:], dqq[:],
                                                 start=True, stop=True))

                @blk.sync
                def _(sync):
                    sync.wait_ge(dRI.h, DRI0)
                    sync.wait_ge(sQKe.h, SQKE_END)
                    sync.wait_ge(rB.h, RB0)
                    sync.wait_ge(rM.h, 8)
                    sync.wait_ge(gSG.h, 2)
                    for row in (16, 80):
                        dRI.inc(sync.dma_start(out=kt2[row:row + 1, :],
                                               in_=onesrow.ap()), 16)
                    sync.wait_ge(dRI.h, dRI.n)
                    for h in range(2):
                        for q16 in range(16):
                            k = 16 * h + q16
                            sync.wait_ge(rC.h, rc_cme(h))
                            if k >= 2:
                                sync.wait_ge(rC.h, rc_mx(k - 2))
                            dRI.inc(sync.dma_start(out=srow[0:1, k % 2, :],
                                                   in_=cm_t[4 * q16:4 * q16 + 4, h, :]), 16)
                            sync.wait_ge(dRI.h, dRI.n)
                            if k >= 1:
                                kp = k - 1
                                hp, qp = kp // 16, kp % 16
                                sync.wait_ge(rC.h, rc_negc(kp))
                                dRI.inc(sync.dma_start(
                                    out=qt2[64 * hp + 16:64 * hp + 17,
                                            512 * qp:512 * (qp + 1)],
                                    in_=negc[0:1, kp % 2, :]), 16)
                                sync.wait_ge(dRI.h, dRI.n)
                    sync.wait_ge(rC.h, rc_negc(31))
                    dRI.inc(sync.dma_start(
                        out=qt2[64 + 16:64 + 17, 512 * 15:512 * 16],
                        in_=negc[0:1, 31 % 2, :]), 16)
                    sync.wait_ge(dRI.h, dRI.n)

                NCH = NSB * KC

                @blk.gpsimd
                def _(gpsimd):
                    for s in range(NSB):
                        gpsimd.wait_ge(rC.h, rc_negc(s))
                        pBC.inc(gpsimd.partition_broadcast(
                            nc_bc[0:128, s, :], negc[0:1, s % 2, :], 128))

                @blk.tensor
                def _(tensor):
                    tensor.wait_ge(dRI.h, DRI_PRE_END)
                    tensor.wait_ge(rC.h, RC_PRE_END)

                    def vagg(gp):
                        sbp, jp = divmod(gp, KC)
                        hp = sbp // 16
                        if jp == 0 and sbp >= 2:
                            tensor.wait_ge(aEV.h, 2 * (sbp - 1))
                        tensor.matmul(p_att[sbp % 2][:],
                                      vsb[:, jp, 128 * hp:128 * (hp + 1)],
                                      ber[:, gp % 4, :],
                                      start=(jp == 0), stop=(jp == KC - 1))
                        pBV.inc(tensor.matmul(p_z[sbp % 2][:], c_o128bf[:], ber[:, gp % 4, :],
                                              start=(jp == 0), stop=(jp == KC - 1)))

                    for G in range(0, NCH, 2):
                        sb = G // KC
                        hh, qsb = sb // 16, sb % 16
                        if G >= 4:
                            tensor.wait_ge(pEX.h, G - 2)
                        for g in (G, G + 1):
                            j = g % KC
                            pSC.inc(tensor.matmul(
                                p_sc[g % 4][:],
                                kt2[64 * hh:64 * hh + 17, 128 * j:128 * (j + 1)],
                                qt2[64 * hh:64 * hh + 17, 512 * qsb:512 * (qsb + 1)],
                                start=True, stop=True))
                        if G >= 4:
                            tensor.wait_ge(pLK.h, G - 2)
                            vagg(G - 4)
                            vagg(G - 3)
                    for gp in range(NCH - 4, NCH):
                        tensor.wait_ge(pLK.h, gp + 1)
                        vagg(gp)

                @blk.scalar
                def _(scalar):
                    scalar.wait_ge(dRI.h, DRI_PRE_END)

                    def e08prep(s):
                        scalar.wait_ge(pBC.h, s + 1)
                        if s >= 2:
                            scalar.wait_ge(pLK.h, KC * (s - 1))
                        sE8.inc(scalar.activation(e08c[:, s % 2, :], nc_bc[:, s, :],
                                                  AF.Exp, scale=0.8))

                    e08prep(0)
                    for g in range(NCH):
                        sb, j = divmod(g, KC)
                        if j == 4 and sb < NSB - 1:
                            e08prep(sb + 1)
                        scalar.wait_ge(pSC.h, g + 1)
                        if g >= 2:
                            scalar.wait_ge(pLK.h, g - 1)
                        scalar.activation(eb1[:, g % 2, :], p_sc[g % 4][:], AF.Exp)
                        pEX.inc(scalar.activation(eb2[:, g % 2, :], p_sc[g % 4][:],
                                                  AF.Exp, scale=0.2))

                @blk.vector
                def _(vector):
                    def evac(s):
                        vector.wait_ge(pBV.h, KC * (s + 1))
                        if s >= 2:
                            vector.wait_ge(dAT.h, 16 * (s - 1))
                        aEV.inc(vector.tensor_copy(att_st[:, s % 2, :], p_att[s % 2][:]))
                        if s >= 1:
                            vector.wait_ge(zTK.h, 16 * s)
                        aEV.inc(vector.tensor_copy(zst[:], p_z[s % 2][:]))

                    for g in range(NCH):
                        sb, j = divmod(g, KC)
                        if j == 0:
                            vector.wait_ge(sE8.h, sb + 1)
                        if j == 3 and sb >= 1:
                            evac(sb - 1)
                        vector.wait_ge(pEX.h, g + 1)
                        if g >= 4:
                            vector.wait_ge(pBV.h, g - 3)
                        qv.inc(vector.tensor_tensor(
                            eb2[:, g % 2, :], eb2[:, g % 2, :], e08c[:, sb % 2, :],
                            ALU.mult))
                        vector.wait_ge(qv.h, qv.n)
                        pLK.inc(vector.tensor_tensor(
                            ber[:, g % 4, :], eb1[:, g % 2, :], eb2[:, g % 2, :],
                            ALU.max))
                    evac(NSB - 1)
                AEV_ALL = aEV.n

                @blk.sync
                def _(sync):
                    for sb in range(NSB):
                        hh, qsb = sb // 16, sb % 16
                        sync.wait_ge(aEV.h, 2 * sb + 1)
                        dAT.inc(sync.dma_start(
                            out=ag_att_i.ap()[128 * hh:128 * (hh + 1),
                                              512 * qsb:512 * (qsb + 1)],
                            in_=att_st[:, sb % 2, :]), 16)
                        sync.wait_ge(dAT.h, dAT.n)
                        sync.wait_ge(aEV.h, 2 * sb + 2)
                        zTK.inc(sync.dma_start(
                            out=ztmp[64 * hh + 4 * qsb:64 * hh + 4 * qsb + 4, :],
                            in_=zst[0:1, :]), 16)
                        sync.wait_ge(zTK.h, zTK.n)
                DAT_END = dAT.n
                ZTK_DMAS = zTK.n

                @blk.tensor
                def _(tensor):
                    tensor.wait_ge(zTK.h, ZTK_DMAS)
                    zTK.inc(tensor.transpose(p_att[0][:, 0:128], ztmp[:], c_id32[:]))

                @blk.vector
                def _(vector):
                    vector.wait_ge(zTK.h, ZTK_DMAS + 1)
                    zTK.inc(vector.reciprocal(ztok[:], p_att[0][:, 0:128]))

                @blk.sync
                def _(sync):
                    sync.wait_ge(dS.h, dS.n)
                    sync.wait_ge(zTK.h, ZTK_DMAS + 2)
                    dst = ag_zi_i.ap().rearrange("(h p) c -> p h c", h=2, p=128)
                    srcz = ztok[:].rearrange("p (h c) -> p h c", h=2)
                    dS.inc(sync.dma_start(out=dst, in_=srcz), 16)
                ZI_FEED = dS.n

                @blk.gpsimd
                def _(gpsimd):
                    gpsimd.wait_ge(dAT.h, DAT_END)
                    gpsimd.wait_ge(dS.h, ZI_FEED)
                    ccS.inc(gpsimd.collective_compute(
                        "AllGather", ALU.bypass, replica_groups=RG,
                        ins=[ag_att_i.ap().opt()], outs=[ag_att.ap().opt()]))
                    ccS.inc(gpsimd.collective_compute(
                        "AllGather", ALU.bypass, replica_groups=RG,
                        ins=[ag_zi_i.ap().opt()], outs=[ag_zi.ap().opt()]))
                ATT_CC = ccS.n

        with ExitStack() as phF:
            rF = phF.enter_context
            w1b = rF(nc.sbuf_tensor("w1b", [128, 8, FF], BF16))
            w2b = rF(nc.sbuf_tensor("w2b", [128, 32, D], BF16))
            hb = rF(nc.sbuf_tensor("hb", [128, 2, 256], BF16))
            ystg = rF(nc.sbuf_tensor("ystg", [128, 2, 2, D], BF16))
            p_h = [rF(nc.psum_tensor(f"p_h{i}", [128, 256], F32)) for i in range(2)]
            p_y = [rF(nc.psum_tensor(f"p_y{i}", [128, 512], F32)) for i in range(4)]

            @blk.gpsimd
            def _(gpsimd):
                for h in range(2):
                    dG.inc(gpsimd.dma_gather(
                        xint2[:, h], ag_xn.ap(), idx16[1][0:128, 32 * h:32 * (h + 1)],
                        num_idxs=512, num_idxs_reg=512, elem_size=D, transpose=True), 16)
                dG.inc(gpsimd.dma_start(out=w1b[:], in_=w1_d[0].rearrange("(c p) f -> p c f", p=128)), 16)
                dG.inc(gpsimd.dma_start(out=w2b[:], in_=w2_d[0].rearrange("(c p) d -> p c d", p=128)), 16)
                gpsimd.wait_ge(fW2.h, 32 * 8)
                dW.inc(gpsimd.dma_start(out=w1b[:], in_=w1_d[1].rearrange("(c p) f -> p c f", p=128)), 16)
                dW.inc(gpsimd.dma_start(out=w2b[:], in_=w2_d[1].rearrange("(c p) d -> p c d", p=128)), 16)
                for h in range(2, 4):
                    dG2.inc(gpsimd.dma_gather(
                        xint[:, h], ag_xn.ap(), idx16[1][0:128, 32 * h:32 * (h + 1)],
                        num_idxs=512, num_idxs_reg=512, elem_size=D, transpose=True), 16)
            DG_END = dG.n
            W0_READY = DG_END

            @blk.tensor
            def _(tensor):
                for x in range(2):
                    if x == 0:
                        tensor.wait_ge(dG.h, W0_READY)
                    else:
                        tensor.wait_ge(dW.h, 32)

                    def ymm(fc, gtb):
                        u = 32 * gtb + fc
                        tensor.wait_ge(fH.h, u + 1)
                        if fc == 0:
                            tensor.wait_ge(fY.h, gtb)
                        for tt in range(2):
                            for dh in range(2):
                                inst = tensor.matmul(
                                    p_y[2 * tt + dh][:],
                                    hb[:, fc % 2, 128 * tt:128 * (tt + 1)],
                                    w2b[:, fc, 512 * dh:512 * (dh + 1)],
                                    start=(fc == 0), stop=(fc == 31))
                        fW2.inc(inst)

                    for tb in range(8):
                        gtb = 8 * x + tb
                        xin = xint2 if (x == 1 and tb < 4) else xint
                        if x == 1 and tb == 4:
                            tensor.wait_ge(dG2.h, 32)
                        for fc in range(32):
                            u = 32 * gtb + fc
                            if u >= 2:
                                tensor.wait_ge(fH.h, u - 1)
                            for dc in range(8):
                                inst = tensor.matmul(
                                    p_h[fc % 2][:], w1b[:, dc, 128 * fc:128 * (fc + 1)],
                                    xin[:, tb // 2, dc, 256 * (tb % 2):256 * (tb % 2) + 256],
                                    start=(dc == 0), stop=(dc == 7))
                                if dc == 7:
                                    fHm.inc(inst)
                            if fc >= 1:
                                ymm(fc - 1, gtb)
                        ymm(31, gtb)
                    fDone.inc(tensor.nop())

            @blk.scalar
            def _(scalar):
                for x in range(2):
                    for tb in range(8):
                        gtb = 8 * x + tb
                        for fc in range(32):
                            u = 32 * gtb + fc
                            scalar.wait_ge(fHm.h, u + 1)
                            if u >= 2:
                                scalar.wait_ge(fW2.h, u - 1)
                            fH.inc(scalar.activation(hb[:, fc % 2, :], p_h[fc % 2][:], AF.Gelu))

            @blk.vector
            def _(vector):
                for x in range(2):
                    for tb in range(8):
                        gtb = 8 * x + tb
                        vector.wait_ge(fW2.h, 32 * (gtb + 1))
                        if gtb >= 2:
                            vector.wait_ge(dFF.h, 32 * gtb)
                        for tt in range(2):
                            for dh in range(2):
                                qv.inc(vector.tensor_copy(
                                    ystg[:, gtb % 2, tt, 512 * dh:512 * (dh + 1)],
                                    p_y[2 * tt + dh][:]))
                        vector.wait_ge(qv.h, qv.n)
                        fY.inc(vector.engine_nop())

            @blk.sync
            def _(sync):
                sync.wait_ge(qv.h, QV_EOFF)
                for x in range(2):
                    dFF.inc(sync.dma_start(out=ag_ffn_i.ap()[x * (CAP + 1):x * (CAP + 1) + 1, :],
                                           in_=zrow[:]), 16)
                sync.wait_ge(dFF.h, dFF.n)
                for x in range(2):
                    for tb in range(8):
                        gtb = 8 * x + tb
                        sync.wait_ge(fY.h, gtb + 1)
                        for tt in range(2):
                            row = x * (CAP + 1) + 1 + 256 * tb + 128 * tt
                            dFF.inc(sync.dma_start(out=ag_ffn_i.ap()[row:row + 128, :],
                                                   in_=ystg[:, gtb % 2, tt, :]), 16)
                        sync.wait_ge(dFF.h, dFF.n)
            DFF_END = dFF.n

            @blk.gpsimd
            def _(gpsimd):
                gpsimd.wait_ge(dFF.h, DFF_END)
                ccS.inc(gpsimd.collective_compute(
                    "AllGather", ALU.bypass, replica_groups=RG,
                    ins=[ag_ffn_i.ap().opt()], outs=[ag_ffn.ap().opt()]))
            FFN_CC = ccS.n

        phW.close()

        with ExitStack() as phS:
            rS = phS.enter_context
            acc = rS(nc.sbuf_tensor("acc", [128, NT, D], F32))
            gbuf = rS(nc.sbuf_tensor("gbuf", [128, 2, 16, D], BF16))
            att8 = rS(nc.sbuf_tensor("att8", [128, 2, 8, 512], BF16))
            ziA = rS(nc.sbuf_tensor("ziA", [128, 8, 16], F32))
            g8 = rS(nc.sbuf_tensor("g8", [128, 8, 16], F32))
            idxA = rS(nc.sbuf_tensor("idxA", [128, 8, 128], I16))
            xnr = rS(nc.sbuf_tensor("xnr", [128, 2, D], F32))
            p_at = [rS(nc.psum_tensor(f"p_at{i}", [128, 128], BF16)) for i in range(2)]

            dSC = Sem(en(nc.semaphore("dSC")))

            @blk.sync
            def _(sync):
                sync.wait_ge(fDone.h, 2)
                sync.wait_ge(dFF.h, DFF_END)
                dAS.inc(sync.dma_start(out=acc[:], in_=x_my.rearrange("(a p) d -> p a d", p=128)), 16)
                dAS.inc(sync.dma_start(out=g8[:], in_=pr_stash.ap()), 16)
                sync.wait_ge(dAS.h, dAS.n)
                for k in range(NT):
                    if k >= 1:
                        sync.wait_ge(aXV.h, max(k - 1, 0))
                    dAS.inc(sync.dma_start(
                        out=xnr[:, k % 2, :],
                        in_=xn_stash.ap()[128 * k:128 * (k + 1), :]), 16)
                    sync.wait_ge(dAS.h, dAS.n)

            @blk.scalar
            def _(scalar):
                scalar.wait_ge(fDone.h, 2)
                scalar.wait_ge(dFF.h, DFF_END)
                scalar.wait_ge(ccS.h, ATT_CC)
                scalar.wait_ge(ccS.h, INV_CC)
                role_sv = scalar.value_load(rolei_s[0:1, 0:1], min_val=0, max_val=3)
                src_zi = ag_zi.ap().rearrange("(h p) c -> p h c", h=8, p=128)
                dSC.inc(scalar.dma_start(
                    out=ziA[:], in_=src_zi[:, :, bass.ts(role_sv, 16)]), 16)
                scalar.wait_ge(dSC.h, dSC.n)
                inv_q = ag_inv.ap()[:, bass.ts(role_sv, 128)]
                for e in range(8):
                    dSC.inc(scalar.dma_start(
                        out=idxA[0:16, e, :],
                        in_=inv_q[16 * e:16 * (e + 1)]), 16)
                    scalar.wait_ge(dSC.h, dSC.n)
                    dSC.inc(scalar.dma_start(out=idxA[16:32, e, :], in_=idxA[0:16, e, :]), 16)
                    scalar.wait_ge(dSC.h, dSC.n)
                    dSC.inc(scalar.dma_start(out=idxA[32:64, e, :], in_=idxA[0:32, e, :]), 16)
                    scalar.wait_ge(dSC.h, dSC.n)
                    dSC.inc(scalar.dma_start(out=idxA[64:128, e, :], in_=idxA[0:64, e, :]), 16)
                    scalar.wait_ge(dSC.h, dSC.n)
                src_at = ag_att.ap().rearrange("(h p) t -> p h t", h=8, p=128)
                ap_qat = src_at[:, :, bass.ts(role_sv, 2048)]
                for m in range(4):
                    if m >= 2:
                        scalar.wait_ge(aTR.h, 32 * (m - 1))
                    dSC.inc(scalar.dma_start(
                        out=att8[:, m % 2, :, :],
                        in_=ap_qat[:, :, 512 * m:512 * (m + 1)]), 16)
                    scalar.wait_ge(dSC.h, dSC.n)
            DSC_PRE = 16 + 64 * 8
            DSC_END = dSC.n

            @blk.tensor
            def _(tensor):
                for k in range(NT):
                    tensor.wait_ge(dSC.h, DSC_PRE + 16 * (k // 4 + 1))
                    for h in range(8):
                        g = 8 * k + h
                        if g >= 2:
                            tensor.wait_ge(aAC.h, g - 1)
                        aTR.inc(tensor.transpose(
                            p_at[g % 2][:],
                            att8[:, (k // 4) % 2, h, 128 * (k % 4):128 * (k % 4 + 1)],
                            c_idbf[:]))

            @blk.vector
            def _(vector):
                vector.wait_ge(dAS.h, 32)
                vector.wait_ge(dSC.h, 16)
                for k in range(NT):
                    vector.wait_ge(dAS.h, 32 + 16 * (k + 1))
                    qv.inc(vector.tensor_add(acc[:, k, :], acc[:, k, :], xnr[:, k % 2, :]))
                    vector.wait_ge(qv.h, qv.n)
                    aXV.inc(vector.engine_nop())
                    for h in range(8):
                        g = 8 * k + h
                        vector.wait_ge(aTR.h, g + 1)
                        aAC.inc(vector.scalar_tensor_tensor(
                            acc[:, k, 128 * h:128 * (h + 1)], p_at[g % 2][:],
                            ziA[:, h, k:k + 1], acc[:, k, 128 * h:128 * (h + 1)],
                            ALU.mult, ALU.add))
                        vector.wait_ge(aAC.h, aAC.n)

            aGp = [Sem(en(nc.semaphore("aGp0"))), Sem(en(nc.semaphore("aGp1")))]

            @blk.gpsimd
            def _(gpsimd):
                gpsimd.wait_ge(ccS.h, FFN_CC)
                gpsimd.wait_ge(dSC.h, DSC_PRE)
                for e in range(8):
                    if e >= 2:
                        gpsimd.wait_ge(aGB.h, e - 1)
                        gpsimd.wait_ge(aGp[e % 2].h, 32 * (e // 2))
                    for hf in range(2):
                        aGp[e % 2].inc(gpsimd.dma_gather(
                            gbuf[:, e % 2, 8 * hf:8 * (hf + 1), :], ag_ffn.ap(),
                            idxA[0:128, e, 64 * hf:64 * (hf + 1)],
                            num_idxs=1024, num_idxs_reg=1024, elem_size=D,
                            transpose=False), 16)

            @blk.vector
            def _(vector):
                for e in range(8):
                    vector.wait_ge(aGp[e % 2].h, 32 * (e // 2 + 1))
                    for k in range(NT):
                        qv.inc(vector.scalar_tensor_tensor(
                            acc[:, k, :], gbuf[:, e % 2, k, :],
                            g8[:, e, k:k + 1], acc[:, k, :],
                            ALU.mult, ALU.add))
                        vector.wait_ge(qv.h, qv.n)
                    aGB.inc(vector.engine_nop())

            @blk.sync
            def _(sync):
                sync.wait_ge(aGB.h, 8)
                dO.inc(sync.dma_start(out=out_my.rearrange("(a p) d -> p a d", p=128),
                                      in_=acc[:]), 16)
                sync.wait_ge(dO.h, 16)

    lower_extended_insts(nc)
    return nc


_NC_CACHE = {}


def _get_nc():
    if "nc" not in _NC_CACHE:
        _NC_CACHE["nc"] = build_nc()
    return _NC_CACHE["nc"]


def make_in_maps(L0_attn_x, L0_attn_Wq, L0_attn_bq, L0_attn_Wk, L0_attn_bk,
                 L0_attn_Wv, L0_attn_bv, L0_attn_gamma, L0_ffn_gate_w,
                 L0_ffn_w1, L0_ffn_w2):
    x = np.ascontiguousarray(np.asarray(L0_attn_x, dtype=np.float32))
    Wq = np.asarray(L0_attn_Wq, dtype=np.float32)
    bq = np.asarray(L0_attn_bq, dtype=np.float32)
    Wk = np.asarray(L0_attn_Wk, dtype=np.float32)
    bk = np.asarray(L0_attn_bk, dtype=np.float32)
    Wv = np.asarray(L0_attn_Wv, dtype=np.float32)
    bv = np.asarray(L0_attn_bv, dtype=np.float32)
    gamma = np.asarray(L0_attn_gamma, dtype=np.float32).reshape(1, 1)
    gw = np.asarray(L0_ffn_gate_w, dtype=np.float32)
    w1 = np.asarray(L0_ffn_w1, dtype=np.float32)
    w2 = np.asarray(L0_ffn_w2, dtype=np.float32)

    in_maps = []
    for c in range(8):
        b, role = c // 4, c % 4
        h0, h1 = 2 * role, 2 * role + 1
        wqT = np.zeros((D, 128), np.float32)
        wqT[:, 0:16] = Wq[16 * h0:16 * h0 + 16].T
        wqT[:, 64:80] = Wq[16 * h1:16 * h1 + 16].T
        wkT = np.zeros((D, 128), np.float32)
        wkT[:, 0:16] = Wk[16 * h0:16 * h0 + 16].T
        wkT[:, 64:80] = Wk[16 * h1:16 * h1 + 16].T
        bqp = np.zeros((128, 1), np.float32)
        bqp[0:16, 0] = bq[16 * h0:16 * h0 + 16]
        bqp[64:80, 0] = bq[16 * h1:16 * h1 + 16]
        bkp = np.zeros((128, 1), np.float32)
        bkp[0:16, 0] = bk[16 * h0:16 * h0 + 16]
        bkp[64:80, 0] = bk[16 * h1:16 * h1 + 16]
        in_maps.append({
            "x_my": np.ascontiguousarray(x[b, Q * role:Q * (role + 1), :]),
            "wqT": np.ascontiguousarray(wqT),
            "wkT": np.ascontiguousarray(wkT),
            "bqp": bqp,
            "bkp": bkp,
            "wvT": np.ascontiguousarray(Wv[256 * role:256 * (role + 1), :].T),
            "bv_my": np.ascontiguousarray(bv[256 * role:256 * (role + 1)].reshape(1, 256)),
            "gamma_in": gamma,
            "rolev": np.array([[float(role)]], np.float32),
            "rolei": np.array([[role]], np.int32),
            "gwT": np.ascontiguousarray(gw.T),
            "w1_0": np.ascontiguousarray(w1[h0]),
            "w1_1": np.ascontiguousarray(w1[h1]),
            "w2_0": np.ascontiguousarray(w2[h0]),
            "w2_1": np.ascontiguousarray(w2[h1]),
        })
    return in_maps


def kernel(**inputs):
    in_maps = make_in_maps(**inputs)
    nc = _get_nc()
    res = run_bass_kernel_spmd(nc, in_maps, core_ids=list(range(8)))
    out = np.empty((2, T, D), np.float32)
    for c in range(8):
        b, role = c // 4, c % 4
        out[b, Q * role:Q * (role + 1), :] = np.asarray(res.results[c]["out_my"])
    return out


